# version 04
# baseline (speedup 1.0000x reference)
import math
import numpy as np
import concourse.bass as bass
import concourse.mybir as mybir
from concourse.bass_utils import run_bass_kernel_spmd

F32 = mybir.dt.float32
BF16 = mybir.dt.bfloat16
U8 = mybir.dt.uint8
AF = mybir.ActivationFunctionType
ALU = mybir.AluOpType

NCORES = 8
T = 4096
D = 1024
NDC = D // 128
DH = 4096
NHC = DH // 128
SCALE_A = float(D) ** -0.25
SCALE_B = 1.0 / math.sqrt(float(D))
EPS = 1e-5
EXP_SHIFT = -20.0

NSUB = 4

ENGS = ("sp", "act", "dve", "pe", "pool")


class Prog:

    def __init__(self):
        self.ops = {e: [] for e in ENGS}
        self.cnt = {}
        self.pending = {e: {} for e in ENGS}

    def sem(self, name):
        assert name not in self.cnt
        self.cnt[name] = 0

    def op(self, eng, method, *args, waits=(), inc=None, tag="", **kwargs):
        w = dict(self.pending[eng])
        self.pending[eng] = {}
        for s, v in waits:
            assert v <= self.cnt[s], (tag, s, v, self.cnt[s])
            if w.get(s, -1) < v:
                w[s] = v
        post = None
        if inc is not None:
            s, n = inc
            self.cnt[s] += n
            post = self.cnt[s]
        self.ops[eng].append((tuple(w.items()), method, args, kwargs, inc, tag))
        return post

    def after(self, eng, waits):
        p = self.pending[eng]
        for s, v in waits:
            if p.get(s, -1) < v:
                p[s] = v

    def barrier(self):
        snap = {s: v for s, v in self.cnt.items() if v > 0}
        for e in ENGS:
            p = self.pending[e]
            for s, v in snap.items():
                if p.get(s, -1) < v:
                    p[s] = v

    def replay(self, eng, proxy, sems):
        for waits, method, args, kwargs, inc, tag in self.ops[eng]:
            for s, v in waits:
                proxy.wait_ge(sems[s], v)
            inst = getattr(proxy, method)(*args, **kwargs)
            if inc is not None:
                assert inst is not None, f"op {tag} returned no instruction"
                inst.then_inc(sems[inc[0]], inc[1])


class Arena:
    def __init__(self, nc, pbytes):
        self.h = nc.alloc_sbuf_tensor("arena", [128, pbytes], U8)
        self.pbytes = pbytes
        self._v = {}

    def view(self, off, npart, cols, dtype):
        isz = mybir.dt.size(dtype)
        assert off % 64 == 0 and off + cols * isz <= self.pbytes, (off, cols, isz)
        key = (off, npart, cols, str(dtype))
        if key not in self._v:
            self._v[key] = self.h.bitcast(dtype).ap()[0:npart, off // isz: off // isz + cols]
        return self._v[key]


class DBuf:

    def __init__(self, P, name, nslots=2):
        self.P, self.name, self.n = P, name, nslots
        self.ld = [f"{name}{i}" for i in range(nslots)]
        for s in self.ld:
            P.sem(s)
        self.issued = [0] * nslots
        self.i = 0

    def load(self, qeng, dst, src, waits=()):
        s = self.i % self.n
        self.i += 1
        post = self.P.op(qeng, "dma_start", out=dst, in_=src, waits=waits,
                         inc=(self.ld[s], 16), tag=self.name)
        self.issued[s] += 1
        assert post == 16 * self.issued[s]
        return (self.ld[s], post)


def build_program(nsub=NSUB):
    nc = bass.Bass("TRN2", target_bir_lowering=False)
    P = Prog()
    for s in ("pe", "act", "dve"):
        P.sem(s)

    xin = nc.dram_tensor("xT", [D, T], F32, kind="ExternalInput").ap()
    wqa = nc.dram_tensor("wqa", [128, 128], F32, kind="ExternalInput").ap()
    wka = nc.dram_tensor("wka", [128, 128], F32, kind="ExternalInput").ap()
    wva = nc.dram_tensor("wva", [128, 128], F32, kind="ExternalInput").ap()
    wut = nc.dram_tensor("wut", [D, D], F32, kind="ExternalInput").ap()
    but = nc.dram_tensor("but", [128, NDC], F32, kind="ExternalInput").ap()
    wqbt = nc.dram_tensor("wqbt", [D, D], F32, kind="ExternalInput").ap()
    wkbt = nc.dram_tensor("wkbt", [D, D], F32, kind="ExternalInput").ap()
    wvbt = nc.dram_tensor("wvbt", [D, D], F32, kind="ExternalInput").ap()
    wobt = nc.dram_tensor("wobt", [D, D], F32, kind="ExternalInput").ap()
    dlog = nc.dram_tensor("dlog", [1, 1], F32, kind="ExternalInput").ap()
    oscl = nc.dram_tensor("oscl", [1, 1], F32, kind="ExternalInput").ap()
    w1t = nc.dram_tensor("w1t", [NHC, 128, NDC * 128], F32, kind="ExternalInput").ap()
    w2t = nc.dram_tensor("w2t", [NHC, 128, NDC * 128], F32, kind="ExternalInput").ap()
    h1t = nc.dram_tensor("h1t", [NHC, 128, NHC * 128], F32, kind="ExternalInput").ap()
    h2t = nc.dram_tensor("h2t", [NHC, 128, NHC * 128], F32, kind="ExternalInput").ap()
    w3t = nc.dram_tensor("w3t", [NDC, 128, NHC * 128], F32, kind="ExternalInput").ap()
    s1v = nc.dram_tensor("s1v", [128, NHC], F32, kind="ExternalInput").ap()
    s2v = nc.dram_tensor("s2v", [128, NHC], F32, kind="ExternalInput").ap()
    sh1v = nc.dram_tensor("sh1v", [128, NHC], F32, kind="ExternalInput").ap()
    sh2v = nc.dram_tensor("sh2v", [128, NHC], F32, kind="ExternalInput").ap()
    s3v = nc.dram_tensor("s3v", [128, NDC], F32, kind="ExternalInput").ap()
    m1c = nc.dram_tensor("m1c", [128, 896], F32, kind="ExternalInput").ap()
    ioc = nc.dram_tensor("ioc", [128, 512], F32, kind="ExternalInput").ap()
    dm1 = nc.dram_tensor("dm1", [1, 32], F32, kind="ExternalInput").ap()
    on1 = nc.dram_tensor("on1", [1, 128], F32, kind="ExternalInput").ap()
    cstv = nc.dram_tensor("cstv", [128, 4], F32, kind="ExternalInput").ap()
    yout = nc.dram_tensor("y", [D, T], F32, kind="ExternalOutput").ap()

    xr1 = nc.dram_tensor("xr1", [D, T], F32, kind="Internal").ap()
    xr2 = nc.dram_tensor("xr2", [D, T], F32, kind="Internal").ap()
    xr3 = nc.dram_tensor("xr3", [D, T], F32, kind="Internal").ap()
    w1s = nc.dram_tensor("w1s", [NHC, 128, NDC * 128], BF16, kind="Internal").ap()
    w2s = nc.dram_tensor("w2s", [NHC, 128, NDC * 128], BF16, kind="Internal").ap()
    h1s = nc.dram_tensor("h1s", [NHC, 128, NHC * 128], BF16, kind="Internal").ap()
    h2s = nc.dram_tensor("h2s", [NHC, 128, NHC * 128], BF16, kind="Internal").ap()
    w3s = nc.dram_tensor("w3s", [NDC, 128, NHC * 128], BF16, kind="Internal").ap()
    wqbs = nc.dram_tensor("wqbs", [D, D], BF16, kind="Internal").ap()
    wkbs = nc.dram_tensor("wkbs", [D, D], BF16, kind="Internal").ap()
    wvbs = nc.dram_tensor("wvbs", [D, D], BF16, kind="Internal").ap()
    wobs = nc.dram_tensor("wobs", [D, D], BF16, kind="Internal").ap()
    qS = nc.dram_tensor("qS", [NDC, 128, T], BF16, kind="Internal").ap()
    kS = nc.dram_tensor("kS", [NDC, 128, T], BF16, kind="Internal").ap()
    vS = nc.dram_tensor("vS", [T, D], BF16, kind="Internal").ap()

    def sbuf(name, p, cols, dtype):
        return nc.alloc_sbuf_tensor(name, [p, cols], dtype).ap()

    t1m = sbuf("t1m", 128, 896, F32)
    t0m = sbuf("t0m", 128, 896, BF16)
    cio = sbuf("cio", 128, 512, F32)
    ldt = sbuf("ldt", 128, 32, F32)
    lndb = sbuf("lndb", 128, 1, F32)
    oscb = sbuf("oscb", 128, 1, F32)
    onbf = sbuf("onbf", 128, 1, BF16)
    onf = sbuf("onf", 1, 128, F32)
    busb = sbuf("busb", 128, NDC, F32)
    s1b = sbuf("s1b", 128, NHC, F32)
    s2b = sbuf("s2b", 128, NHC, F32)
    sh1b = sbuf("sh1b", 128, NHC, F32)
    sh2b = sbuf("sh2b", 128, NHC, F32)
    s3b = sbuf("s3b", 128, NDC, F32)
    wqab = sbuf("wqab", 128, 128, BF16)
    wkab = sbuf("wkab", 128, 128, BF16)
    wvab = sbuf("wvab", 128, 128, BF16)
    sc0 = sbuf("sc0", 1, 8, F32)
    dm1s = sbuf("dm1s", 1, 32, F32)
    wstg = sbuf("wstg", 128, 3 * 128, F32)
    cstb = sbuf("cstb", 128, 4, F32)

    ARENA = 196 * 1024
    A = Arena(nc, ARENA)
    ps = [nc.alloc_psum_tensor(f"ps{i}", [128, 1024], F32).ap() for i in range(4)]

    def bank(i):
        return ps[i // 2][:, (i % 2) * 512:(i % 2) * 512 + 512]

    def pair(i):
        return ps[i]

    cl = DBuf(P, "cl", 2)
    cs = DBuf(P, "cs", 2)
    xl = DBuf(P, "xl", 2)
    xs = DBuf(P, "xs", 2)
    xq = DBuf(P, "xq", 2)
    wl = DBuf(P, "wl", 2)
    w2l = DBuf(P, "w2l", 2)
    kl = DBuf(P, "kl", 2)
    vl = DBuf(P, "vl", 2)
    ql = DBuf(P, "ql", 1)
    st = DBuf(P, "st", 2)
    qst = DBuf(P, "qst", 2)

    P.sem("c0")
    for dst_ap, src_ap in (
        (t1m[:, :], m1c[:, :]), (cio[:, :], ioc[:, :]), (onf[:, :], on1[:, :]),
        (busb[:, :], but[:, :]), (s1b[:, :], s1v[:, :]), (s2b[:, :], s2v[:, :]),
        (sh1b[:, :], sh1v[:, :]), (sh2b[:, :], sh2v[:, :]), (s3b[:, :], s3v[:, :]),
        (sc0[0:1, 0:1], dlog[:, :]), (sc0[0:1, 1:2], oscl[:, :]),
        (dm1s[0:1, :], dm1[:, :]), (wstg[:, 0:128], wqa[:, :]),
        (wstg[:, 128:256], wka[:, :]), (wstg[:, 256:384], wva[:, :]),
        (cstb[:, :], cstv[:, :]),
    ):
        P.op("sp", "dma_start", out=dst_ap, in_=src_ap, inc=("c0", 16), tag="c0")
    CV = [("c0", P.cnt["c0"])]

    P.op("dve", "memset", onbf[:, :], 1.0, tag="onbf")
    P.op("dve", "tensor_scalar", t0m[:, :], t1m[:, :], -1.0, 1.0, ALU.mult, ALU.add,
         waits=CV, tag="t0m")
    a0 = P.op("act", "activation", out=sc0[0:1, 2:3], in_=sc0[0:1, 0:1],
              func=AF.Sigmoid, waits=CV, inc=("act", 1), tag="sig")
    d0 = P.op("dve", "tensor_scalar_max", sc0[0:1, 3:4], sc0[0:1, 2:3], 1e-38,
              waits=[("act", a0)], inc=("dve", 1), tag="sigm")
    a1 = P.op("act", "activation", out=sc0[0:1, 4:5], in_=sc0[0:1, 3:4], func=AF.Ln,
              waits=[("dve", d0)], inc=("act", 1), tag="lnd")
    p1 = P.op("pe", "matmul", bank(0)[:, 0:1], onf[0:1, :], sc0[0:1, 4:5],
              start=True, stop=True, waits=[("act", a1)], inc=("pe", 1), tag="bc1")
    p2 = P.op("pe", "matmul", bank(1)[:, 0:1], onf[0:1, :], sc0[0:1, 1:2],
              start=True, stop=True, waits=CV, inc=("pe", 1), tag="bc2")
    p3 = P.op("pe", "matmul", bank(2)[:, 0:32], onf[0:1, :], dm1s[0:1, :],
              start=True, stop=True, waits=CV, inc=("pe", 1), tag="bc3")
    P.op("dve", "tensor_copy", lndb[:, :], bank(0)[:, 0:1], waits=[("pe", p1)],
         tag="lndc")
    P.op("dve", "tensor_copy", oscb[:, :], bank(1)[:, 0:1], waits=[("pe", p2)],
         tag="oscc")
    d3 = P.op("dve", "tensor_scalar", ldt[:, :], bank(2)[:, 0:32], lndb[:, 0:1], None,
              ALU.mult, waits=[("pe", p3)], inc=("dve", 1), tag="ldt")
    oscb_ready = d3
    P.op("dve", "tensor_scalar_mul", wqab[:, :], wstg[:, 0:128], SCALE_A,
         waits=CV, tag="wqab")
    P.op("dve", "tensor_scalar_mul", wkab[:, :], wstg[:, 128:256], SCALE_A,
         tag="wkab")
    P.op("act", "activation", out=wvab[:, :], in_=wstg[:, 256:384], func=AF.Copy,
         waits=CV, inc=("act", 1), tag="wvab")

    wu_sb = A.view(0, 128, NDC * 1024, BF16)
    CF, CB = 16 * 1024, 48 * 1024
    cl_rel = {}
    st_rel = {}

    def conv(src_ap, dst_ap, cols, scale=None, scale_ap=None, to_sbuf=None,
             extra_waits=()):
        s = cl.i % 2
        assert cols * 4 <= 16 * 1024
        cf = A.view(CF + s * 16 * 1024, 128, cols, F32)
        lw = cl.load("sp", cf, src_ap,
                     waits=list(extra_waits) + cl_rel.get(s, []))
        scl = scale_ap if scale_ap is not None else scale
        out_ap = to_sbuf if to_sbuf is not None else A.view(CB + s * 8 * 1024, 128,
                                                            cols, BF16)
        wst = ([] if to_sbuf is not None else st_rel.get(s, []))
        if scl is None:
            post = P.op("act", "activation", out=out_ap, in_=cf, func=AF.Copy,
                        waits=[lw] + wst, inc=("act", 1), tag="cv")
            cl_rel[s] = [("act", post)]
        else:
            post = P.op("dve", "tensor_scalar_mul", out_ap, cf, scl,
                        waits=[lw] + wst, inc=("dve", 1), tag="cvs")
            cl_rel[s] = [("dve", post)]
        if to_sbuf is None:
            sw = cs.load("pool", dst_ap, out_ap, waits=[cl_rel[s][0]])
            st_rel[s] = [sw]

    for ic in range(NDC):
        conv(wut[ic * 128:(ic + 1) * 128, :], None, D,
             to_sbuf=wu_sb[:, ic * 1024:(ic + 1) * 1024])
    P.barrier()

    pieces = []
    if nsub >= 2:
        for oc in range(NHC):
            pieces.append((w1t[oc, :, :], w1s[oc, :, :], None, None))
            pieces.append((w2t[oc, :, :], w2s[oc, :, :], None, None))
        for oc in range(NHC):
            for qq in range(4):
                c = slice(qq * 1024, (qq + 1) * 1024)
                pieces.append((h1t[oc, :, c], h1s[oc, :, c], None, None))
                pieces.append((h2t[oc, :, c], h2s[oc, :, c], None, None))
        for oc in range(NDC):
            for qq in range(4):
                c = slice(qq * 1024, (qq + 1) * 1024)
                pieces.append((w3t[oc, :, c], w3s[oc, :, c], None, None))
    if nsub >= 3:
        for ic in range(NDC):
            sl = slice(ic * 128, (ic + 1) * 128)
            pieces.append((wqbt[sl, :], wqbs[sl, :], cstb[:, 3:4], None))
            pieces.append((wkbt[sl, :], wkbs[sl, :], None, None))
            pieces.append((wvbt[sl, :], wvbs[sl, :], None, None))
            pieces.append((wobt[sl, :], wobs[sl, :], oscb[:, 0:1],
                           [("dve", oscb_ready)]))
    c2 = DBuf(P, "c2", 3)
    c2_rel = {}
    cs_rel = {}
    pieces_i = [0]

    def emit_pieces(cf_off, cb_off, n):
        for _ in range(n):
            i = pieces_i[0]
            if i >= len(pieces):
                return
            pieces_i[0] = i + 1
            src_ap, dst_ap, scale_ap, extra = pieces[i]
            s3 = c2.i % 3
            cf = A.view(cf_off + s3 * 4096, 128, 1024, F32)
            lw = c2.load("sp", cf, src_ap,
                         waits=(extra or []) + c2_rel.get(s3, []))
            s2 = cs.i % 2
            cb = A.view(cb_off + s2 * 2048, 128, 1024, BF16)
            if scale_ap is None:
                post = P.op("dve", "tensor_copy", cb, cf,
                            waits=[lw] + cs_rel.get(s2, []), inc=("dve", 1),
                            tag="cvp")
            else:
                post = P.op("dve", "tensor_scalar_mul", cb, cf, scale_ap,
                            waits=[lw] + cs_rel.get(s2, []), inc=("dve", 1),
                            tag="cvps")
            c2_rel[s3] = [("dve", post)]
            sw = cs.load("pool", dst_ap, cb, waits=[("dve", post)])
            cs_rel[s2] = [sw]

    def ln_stats(src, t0, tcols, sm, sr, xfv, xbv, xqv, tev, rel):
        nh = tcols // 512
        assert nh in (1, 2)
        for dc in range(NDC):
            s = dc % 2
            lw = xs.load("pool", xfv[s], src[dc * 128:(dc + 1) * 128, t0:t0 + tcols],
                         waits=rel.get(("f", s), []))
            ap = P.op("act", "activation", out=xqv[s], in_=xfv[s], func=AF.Square,
                      waits=[lw] + rel.get(("q", s), []), inc=("act", 1), tag="lnsq")
            dp = P.op("dve", "tensor_copy", xbv[s], xfv[s],
                      waits=[lw] + rel.get(("b", s), []), inc=("dve", 1), tag="lncp")
            rel[("f", s)] = [("act", ap), ("dve", dp)]
            pb = pq = None
            for h in range(nh):
                c = slice(h * 512, (h + 1) * 512)
                pb = P.op("pe", "matmul", pair(2)[0:1, c], onbf[:, :], xbv[s][:, c],
                          start=(dc == 0), stop=(dc == NDC - 1), waits=[("dve", dp)],
                          inc=("pe", 1), tag="ln_sx")
                pq = P.op("pe", "matmul", pair(3)[0:1, c], onbf[:, :], xqv[s][:, c],
                          start=(dc == 0), stop=(dc == NDC - 1), waits=[("act", ap)],
                          inc=("pe", 1), tag="ln_sq")
            rel[("b", s)] = [("pe", pb)]
            rel[("q", s)] = [("pe", pq)]
        pw = [("pe", P.cnt["pe"])]
        dm = P.op("dve", "tensor_scalar_mul", sm, pair(2)[0:1, 0:tcols], 1.0 / D,
                  waits=pw, inc=("dve", 1), tag="ln_m")
        P.op("dve", "tensor_scalar_mul", tev[0:1, 0:tcols], pair(3)[0:1, 0:tcols],
             1.0 / D, tag="ln_e")
        a2 = P.op("act", "activation", out=tev[0:1, tcols:2 * tcols], in_=sm,
                  func=AF.Square, waits=pw + [("dve", dm)], inc=("act", 1),
                  tag="ln_ms")
        d3 = P.op("dve", "tensor_sub", tev[0:1, 0:tcols], tev[0:1, 0:tcols],
                  tev[0:1, tcols:2 * tcols], waits=[("act", a2)], inc=("dve", 1),
                  tag="ln_v")
        a3 = P.op("act", "activation", out=tev[0:1, tcols:2 * tcols],
                  in_=tev[0:1, 0:tcols], func=AF.Sqrt, bias=cstb[0:1, 0:1],
                  waits=[("dve", d3)], inc=("act", 1), tag="ln_s")
        d4 = P.op("dve", "reciprocal", sr, tev[0:1, tcols:2 * tcols],
                  waits=[("act", a3)], inc=("dve", 1), tag="ln_r")
        return d4

    def ln_bcast(sm, sr, mb, rb, tcols, ready):
        nh = tcols // 512
        last = None
        for h in range(nh):
            c = slice(h * 512, (h + 1) * 512)
            w = [("dve", ready)] + ([("dve", last)] if last else [])
            pm = P.op("pe", "matmul", bank(4), onf[0:1, :], sm[0:1, c],
                      start=True, stop=True, waits=w, inc=("pe", 1), tag="bcm")
            P.op("dve", "tensor_copy", mb[:, c], bank(4), waits=[("pe", pm)],
                 tag="bcmc")
            pr = P.op("pe", "matmul", bank(5), onf[0:1, :], sr[0:1, c],
                      start=True, stop=True, waits=[("dve", ready)], inc=("pe", 1),
                      tag="bcr")
            last = P.op("dve", "tensor_copy", rb[:, c], bank(5), waits=[("pe", pr)],
                        inc=("dve", 1), tag="bcrc")
        return last

    def emit_attn_a(src, dst):
        import collections
        mb = A.view(16 * 1024, 128, T, F32)
        rb = A.view(32 * 1024, 128, T, F32)
        yT = A.view(48 * 1024, 128, NDC * T, BF16)
        ln_dc = A.view(112 * 1024, 128, T, BF16)
        qT = A.view(120 * 1024, 128, T, BF16)
        kT = A.view(128 * 1024, 128, T, BF16)
        v2 = A.view(136 * 1024, 128, 32 * 130, BF16)
        v2r = v2.rearrange("p (a b) -> p a b", b=130)
        xfh = [A.view(145 * 1024 + s * 8 * 1024, 128, 2048, F32) for s in range(2)]
        prb = [A.view(177 * 1024 + s * 2048, 128, 1024, BF16) for s in range(3)]
        tmpf = A.view(183 * 1024, 128, 512, F32)
        dnv2 = A.view(185 * 1024, 1, 1024, F32)
        rbsAB = A.view(189 * 1024, 128, 512, F32)
        tev = A.view(183 * 1024, 1, 1024, F32)
        sxq = [A.view(161 * 1024 + s * 2048, 128, 512, BF16) for s in range(2)]
        sxb = [A.view(165 * 1024 + s * 1024, 128, 512, BF16) for s in range(2)]
        sxf = [A.view(173 * 1024 + s * 2048, 128, 512, F32) for s in range(2)]
        smcr = A.view(189 * 1024, 1, 1024, F32)
        smc = smcr[0:1, 0:512]
        src_c = smcr[0:1, 512:1024]

        lrel = {}
        smcr2 = [smc, src_c]

        def stats_steps(tc):
            t0s = tc * 512
            par = tc % 2
            bx, bq2 = bank(4 + 2 * par), bank(5 + 2 * par)
            stt = {}
            out = []

            def dc_step(dcs):
                def f():
                    s = dcs % 2
                    lw = xs.load("pool", sxf[s],
                                 src[dcs * 128:(dcs + 1) * 128, t0s:t0s + 512],
                                 waits=lrel.get(("f", s), []))
                    ap = P.op("act", "activation", out=sxq[s], in_=sxf[s],
                              func=AF.Square, waits=[lw] + lrel.get(("q", s), []),
                              inc=("act", 1), tag="lnsq")
                    dp = P.op("dve", "tensor_copy", sxb[s], sxf[s],
                              waits=[lw] + lrel.get(("b", s), []),
                              inc=("dve", 1), tag="lncp")
                    lrel[("f", s)] = [("act", ap), ("dve", dp)]
                    pb = P.op("pe", "matmul", bx[0:1, :], onbf[:, :], sxb[s],
                              start=(dcs == 0), stop=(dcs == NDC - 1),
                              waits=[("dve", dp)], inc=("pe", 1), tag="ln_sx")
                    pq = P.op("pe", "matmul", bq2[0:1, :], onbf[:, :], sxq[s],
                              start=(dcs == 0), stop=(dcs == NDC - 1),
                              waits=[("act", ap)], inc=("pe", 1), tag="ln_sq")
                    lrel[("b", s)] = [("pe", pb)]
                    lrel[("q", s)] = [("pe", pq)]
                    if dcs == NDC - 1:
                        stt["pelast"] = max(pb, pq)
                return f

            def math():
                pw = [("pe", stt["pelast"])] + lrel.get(("m", par), []) + \
                    lrel.get("bcg", [])
                dm = P.op("dve", "tensor_scalar_mul", smcr2[0][0:1, :], bx[0:1, :],
                          1.0 / D, waits=pw, inc=("dve", 1), tag="ln_m")
                P.op("dve", "tensor_scalar_mul", tev[0:1, 0:512], bq2[0:1, :],
                     1.0 / D, tag="ln_e")
                a2 = P.op("act", "activation", out=tev[0:1, 512:1024],
                          in_=smcr2[0][0:1, :], func=AF.Square,
                          waits=pw + [("dve", dm)], inc=("act", 1), tag="ln_ms")
                d3 = P.op("dve", "tensor_sub", tev[0:1, 0:512], tev[0:1, 0:512],
                          tev[0:1, 512:1024], waits=[("act", a2)], inc=("dve", 1),
                          tag="ln_v")
                a3 = P.op("act", "activation", out=tev[0:1, 512:1024],
                          in_=tev[0:1, 0:512], func=AF.Sqrt, bias=cstb[0:1, 0:1],
                          waits=[("dve", d3)], inc=("act", 1), tag="ln_s")
                d4 = P.op("dve", "reciprocal", smcr2[1][0:1, :], tev[0:1, 512:1024],
                          waits=[("act", a3)], inc=("dve", 1), tag="ln_r")
                stt["rdy"] = d4

            def bc():
                c = slice(t0s, t0s + 512)
                pm = P.op("pe", "matmul", bx, onf[0:1, :], smcr2[0][0:1, :],
                          start=True, stop=True, waits=[("dve", stt["rdy"])],
                          inc=("pe", 1), tag="bcm")
                P.op("dve", "tensor_copy", mb[:, c], bx, waits=[("pe", pm)],
                     tag="bcmc")
                pr = P.op("pe", "matmul", bq2, onf[0:1, :], smcr2[1][0:1, :],
                          start=True, stop=True, waits=[("dve", stt["rdy"])],
                          inc=("pe", 1), tag="bcr")
                last = P.op("dve", "tensor_copy", rb[:, c], bq2, waits=[("pe", pr)],
                            inc=("dve", 1), tag="bcrc")
                lrel[("m", par)] = [("dve", last)]
                lrel["bcg"] = [("pe", pr)]
            return [dc_step(d) for d in range(NDC)], math, bc

        prev = None
        for tc in range(8):
            steps, mth, bcf = stats_steps(tc)
            for j, f in enumerate(steps):
                f()
                if prev is not None:
                    if j == 1:
                        prev[0]()
                    elif j == 3:
                        prev[1]()
            prev = (mth, bcf)
        prev[0]()
        prev[1]()
        P.barrier()

        P.op("dve", "memset", v2r[:, :, 64:65], 1.0, tag="v2o1")
        v2o = P.op("dve", "memset", v2r[:, :, 129:130], 1.0, inc=("dve", 1), tag="v2o2")
        P.after("pe", [("dve", v2o)])

        xrel = {}
        ln_guard = [0]

        def emit_norm_chunk(dc, tc):
            h = tc // 4
            if tc % 4 == 0:
                w = xrel.get(h, []) + ([("pe", ln_guard[0])] if tc == 0 else [])
                xrel[("lw", h)] = xl.load(
                    "pool", xfh[h], src[dc * 128:(dc + 1) * 128,
                                        h * 2048:(h + 1) * 2048], waits=w)
            c = slice(tc * 512, tc * 512 + 512)
            cc = slice((tc % 4) * 512, (tc % 4) * 512 + 512)
            w = [xrel[("lw", h)]]
            if tc == 0:
                w.append(("pe", ln_guard[0]))
            P.op("dve", "tensor_sub", tmpf[:, :], xfh[h][:, cc], mb[:, c],
                 waits=w, tag="a_sb")
            post = P.op("dve", "tensor_mul", ln_dc[:, c], tmpf[:, :], rb[:, c],
                        inc=("dve", 1), tag="a_ml")
            xrel[h] = [("dve", post)]
            return post

        ln_done = None
        for tc in range(8):
            ln_done = emit_norm_chunk(0, tc)
        ln_ready = {0: ln_done}

        pair_rel = {}
        slot_rel = {}
        ybank_rel = []
        abc_rel = {}
        pbc_last = []
        tailq = collections.deque()

        def drain_tail(n):
            for _ in range(min(n, len(tailq))):
                tailq.popleft()()

        for dc in range(NDC):
            last_ln = ln_ready[dc]
            qk_rel = {}
            qk_done = last_ln
            for tc in range(8):
                c = slice(tc * 512, tc * 512 + 512)
                bq, bk = bank(tc % 2), bank(2 + tc % 2)
                pq = P.op("pe", "matmul", bq, wqab[:, :], ln_dc[:, c],
                          start=True, stop=True,
                          waits=[("dve", last_ln)] + qk_rel.get(("q", tc % 2), []),
                          inc=("pe", 1), tag="a_q")
                dq = P.op("act", "activation", out=qT[:, c], in_=bq, func=AF.Copy,
                          waits=[("pe", pq)], inc=("act", 1), tag="a_qc")
                qk_rel[("q", tc % 2)] = [("act", dq)]
                pk = P.op("pe", "matmul", bk, wkab[:, :], ln_dc[:, c],
                          start=True, stop=True, waits=qk_rel.get(("k", tc % 2), []),
                          inc=("pe", 1), tag="a_k")
                dk = P.op("dve", "tensor_copy", kT[:, c], bk, waits=[("pe", pk)],
                          inc=("dve", 1), tag="a_kc")
                qk_rel[("k", tc % 2)] = [("dve", dk)]
                qk_done = dk
                drain_tail(1)
            drain_tail(8)
            v_rel = {}
            v_done = qk_done
            pv = None
            for tb in range(32):
                bv = bank(4 + tb % 2)
                wv = [("dve", last_ln)] + v_rel.get(tb % 2, [])
                if tb < 2 and ybank_rel:
                    wv += ybank_rel
                pv = P.op("pe", "matmul", bv[:, 0:128],
                          ln_dc[:, tb * 128:tb * 128 + 128], wvab[:, :],
                          start=True, stop=True, waits=wv, inc=("pe", 1), tag="a_v")
                dst_v = v2r[:, tb, :].rearrange("p (cc dd) -> p cc dd",
                                                cc=2, dd=65)[:, :, 0:64]
                src_v = bv[:, 0:128].rearrange("p (cc dd) -> p cc dd", cc=2, dd=64)
                dv = P.op("dve", "tensor_copy", dst_v, src_v, waits=[("pe", pv)],
                          inc=("dve", 1), tag="a_vc")
                v_rel[tb % 2] = [("dve", dv)]
                v_done = dv
            ln_guard[0] = pv

            for tc in range(8):
                nkb = 4 * tc + 4
                qc = slice(tc * 512, tc * 512 + 512)
                probs = {}
                ypost = {}
                sB = {}

                def emit_scores(kb):
                    pp = pair(kb % 2)
                    kc = slice(kb * 128, kb * 128 + 128)
                    wv = [("dve", max(qk_done, v_done))] + pair_rel.get(kb % 2, [])
                    P.op("pe", "matmul", pp[:, 0:512], kT[0:64, kc], qT[0:64, qc],
                         start=True, stop=True, waits=wv, tag="a_sA")
                    sB[kb] = P.op("pe", "matmul", pp[:, 512:1024], kT[64:128, kc],
                                  qT[64:128, qc], start=True, stop=True,
                                  inc=("pe", 1), tag="a_sB")

                def emit_probs(kb):
                    pp = pair(kb % 2)
                    slot = prb[kb % 3]
                    wv = [("pe", sB[kb])] + slot_rel.get(kb % 3, [])
                    apost = P.op("act", "activation", out=slot[:, :], in_=pp[:, :],
                                 func=AF.Exp, bias=cstb[:, 1:2], waits=wv,
                                 inc=("act", 1), tag="a_ex")
                    pair_rel[kb % 2] = [("act", apost)]
                    if kb < 4 * tc:
                        probs[kb] = (apost, None)
                    else:
                        c0 = 384 - (kb * 128 - tc * 512)
                        P.op("dve", "tensor_mul", slot[:, 0:512], slot[:, 0:512],
                             t0m[:, c0:c0 + 512], waits=[("act", apost)], tag="a_mA")
                        dpost = P.op("dve", "tensor_mul", slot[:, 512:1024],
                                     slot[:, 512:1024], t0m[:, c0:c0 + 512],
                                     inc=("dve", 1), tag="a_mB")
                        probs[kb] = (apost, dpost)

                def emit_y(kb):
                    slot = prb[kb % 3]
                    wv = [("act", probs[kb][0])]
                    if probs[kb][1] is not None:
                        wv.append(("dve", probs[kb][1]))
                    if kb == 0 and ybank_rel:
                        wv += ybank_rel
                    P.op("pe", "matmul", bank(4)[0:65, :],
                         v2[:, kb * 130:kb * 130 + 65], slot[:, 0:512],
                         start=(kb == 0), stop=(kb == nkb - 1), waits=wv, tag="a_yA")
                    ypost[kb] = P.op("pe", "matmul", bank(5)[0:65, :],
                                     v2[:, kb * 130 + 65:kb * 130 + 130],
                                     slot[:, 512:1024],
                                     start=(kb == 0), stop=(kb == nkb - 1),
                                     inc=("pe", 1), tag="a_yB")
                    slot_rel[kb % 3] = [("pe", ypost[kb])]

                for kb in range(nkb):
                    emit_scores(kb)
                    emit_probs(kb)
                    drain_tail(2)
                    if kb == 2:
                        if dc + 1 < NDC:
                            p = emit_norm_chunk(dc + 1, tc)
                            if tc == 7:
                                ln_ready[dc + 1] = p
                        emit_pieces(161 * 1024, 173 * 1024, 3)
                    if kb >= 2:
                        emit_y(kb - 2)
                emit_y(nkb - 2)
                emit_y(nkb - 1)

                def make_tail(dc=dc, tc=tc, qc=qc, ystop=ypost[nkb - 1]):
                    def dn(hd):
                        def f():
                            yb = bank(4 + hd)
                            post = P.op("dve", "reciprocal",
                                        dnv2[0:1, hd * 512:hd * 512 + 512],
                                        yb[64:65, :],
                                        waits=[("pe", ystop)] + (pbc_last if hd == 0 else []),
                                        inc=("dve", 1), tag="a_dn")
                            tail_state[("dn", hd)] = post
                        return f

                    def pbc(hd):
                        def f():
                            bc = bank(6 + hd)
                            post = P.op("pe", "matmul", bc[0:64, :], onf[0:1, 0:64],
                                        dnv2[0:1, hd * 512:hd * 512 + 512],
                                        start=True, stop=True,
                                        waits=[("dve", tail_state[("dn", hd)])],
                                        inc=("pe", 1), tag="a_db")
                            tail_state[("pbc", hd)] = post
                            if hd == 1:
                                pbc_last[:] = [("pe", post)]
                        return f

                    def abc(hd):
                        def f():
                            bc = bank(6 + hd)
                            rbs = rbsAB[hd * 64:hd * 64 + 64, :]
                            post = P.op("act", "activation", out=rbs, in_=bc[0:64, :],
                                        func=AF.Copy,
                                        waits=[("pe", tail_state[("pbc", hd)])] +
                                        abc_rel.get(hd, []),
                                        inc=("act", 1), tag="a_dc")
                            tail_state[("abc", hd)] = post
                        return f

                    def dty(hd):
                        def f():
                            yb = bank(4 + hd)
                            rbs = rbsAB[hd * 64:hd * 64 + 64, :]
                            ydst = yT[:, dc * T:dc * T + T][hd * 64:hd * 64 + 64, qc]
                            post = P.op("dve", "tensor_mul", ydst, yb[0:64, :], rbs,
                                        waits=[("act", tail_state[("abc", hd)])],
                                        inc=("dve", 1), tag="a_ye")
                            abc_rel[hd] = [("dve", post)]
                            tail_state[("dty", hd)] = post
                            if hd == 1:
                                ybank_rel[:] = [("dve", tail_state[("dty", 0)]),
                                                ("dve", post)]
                        return f

                    return [dn(0), dn(1), pbc(0), pbc(1), abc(0),
                            abc(1), dty(0), dty(1)]

                tail_state = {}
                tailq.extend(make_tail())
                drain_tail(2)
        drain_tail(99)

        P.barrier()
        otf = [A.view(145 * 1024 + s * 2048, 128, 512, F32) for s in range(2)]
        xrf = [A.view(149 * 1024 + s * 2048, 128, 512, F32) for s in range(2)]
        ev = {}
        ot_rel = {}
        xr_rel = {}
        i = 0
        for dcO in range(NDC):
            for tc in range(8):
                qc = slice(tc * 512, tc * 512 + 512)
                s = i % 2
                bnk = bank(i % 4)
                p1 = None
                for ic in range(NDC):
                    p1 = P.op("pe", "matmul", bnk,
                              wu_sb[:, ic * 1024 + dcO * 128: ic * 1024 + dcO * 128 + 128],
                              yT[:, ic * T:ic * T + T][:, qc],
                              start=(ic == 0), stop=(ic == NDC - 1),
                              waits=ev.get(i % 4, []) if ic == 0 else (),
                              inc=("pe", 1) if ic == NDC - 1 else None, tag="a_wu")
                lw = xq.load("pool", xrf[s], src[dcO * 128:(dcO + 1) * 128, qc],
                             waits=xr_rel.get(s, []))
                dp = P.op("dve", "scalar_tensor_tensor", otf[s][:, :], bnk,
                          busb[:, dcO:dcO + 1], xrf[s][:, :], ALU.add, ALU.add,
                          waits=[("pe", p1), lw] + ot_rel.get(s, []),
                          inc=("dve", 1), tag="a_e")
                ev[i % 4] = [("dve", dp)]
                sw = st.load("pool", dst[dcO * 128:(dcO + 1) * 128, qc], otf[s][:, :],
                             waits=[("dve", dp)])
                ot_rel[s] = [sw]
                xr_rel[s] = [("dve", dp)]
                emit_pieces(161 * 1024, 173 * 1024, 2)
                i += 1
        emit_pieces(161 * 1024, 173 * 1024, 10 ** 9)
        P.barrier()

    def emit_ffn(src, dst):
        import collections
        a_sb = A.view(0, 128, NHC * 1024, BF16)
        g1 = A.view(64 * 1024, 128, NHC * 1024, BF16)
        lnv = A.view(128 * 1024, 128, NDC * 1024, BF16)
        smx = A.view(128 * 1024, 1, 1024, F32)
        srx = A.view(132 * 1024, 1, 1024, F32)
        tev = A.view(136 * 1024, 1, 2048, F32)
        mbb = A.view(144 * 1024, 128, 1024, F32)
        rbb = A.view(148 * 1024, 128, 1024, F32)
        w1sl = [A.view(152 * 1024 + s * 2048, 128, 1024, BF16) for s in range(2)]
        w2sl = [A.view(156 * 1024 + s * 2048, 128, 1024, BF16) for s in range(2)]
        hsl = [A.view(160 * 1024 + s * 8192, 128, 4096, BF16) for s in range(2)]
        siv = [A.view(176 * 1024 + s * 1024, 128, 512, BF16) for s in range(2)]
        xfv = [A.view(178 * 1024 + s * 4096, 128, 1024, F32) for s in range(2)]
        sxb = [A.view(186 * 1024 + s * 2048, 128, 1024, BF16) for s in range(2)]
        sxqv = A.view(190 * 1024, 128, 1024, BF16)
        otf = [A.view(192 * 1024 + s * 2048, 128, 512, F32) for s in range(2)]

        bg = collections.deque()

        def drain(n):
            for _ in range(min(n, len(bg))):
                bg.popleft()()

        lrel = {}
        xshare = {}
        ln_ready = {}
        bank_rel = {}

        def queue_stats(tc):
            t0 = tc * 1024
            st = {}

            def dc_step(dc):
                def f():
                    s = dc % 2
                    lw = xs.load("pool", xfv[s],
                                 src[dc * 128:(dc + 1) * 128, t0:t0 + 1024],
                                 waits=xshare.get(s, []))
                    ap = P.op("act", "activation", out=sxqv, in_=xfv[s],
                              func=AF.Square, waits=[lw] + lrel.get("qq", []),
                              inc=("act", 1), tag="lnsq")
                    dp = P.op("dve", "tensor_copy", sxb[s], xfv[s],
                              waits=[lw] + lrel.get(("b", s), []),
                              inc=("dve", 1), tag="lncp")
                    xshare[s] = [("act", ap), ("dve", dp)]
                    pb = pq = None
                    for h in range(2):
                        c = slice(h * 512, (h + 1) * 512)
                        pb = P.op("pe", "matmul", pair(2)[0:1, c], onbf[:, :],
                                  sxb[s][:, c], start=(dc == 0), stop=(dc == NDC - 1),
                                  waits=[("dve", dp)], inc=("pe", 1), tag="ln_sx")
                        pq = P.op("pe", "matmul", pair(3)[0:1, c], onbf[:, :],
                                  sxqv[:, c], start=(dc == 0), stop=(dc == NDC - 1),
                                  waits=[("act", ap)], inc=("pe", 1), tag="ln_sq")
                    lrel[("b", s)] = [("pe", pb)]
                    lrel["qq"] = [("pe", pq)]
                    if dc == NDC - 1:
                        st["pelast"] = max(pb, pq)
                return f

            def math():
                pw = [("pe", st["pelast"])]
                dm = P.op("dve", "tensor_scalar_mul", smx[0:1, :], pair(2)[0:1, 0:1024],
                          1.0 / D, waits=pw, inc=("dve", 1), tag="ln_m")
                P.op("dve", "tensor_scalar_mul", tev[0:1, 0:1024],
                     pair(3)[0:1, 0:1024], 1.0 / D, tag="ln_e")
                a2 = P.op("act", "activation", out=tev[0:1, 1024:2048],
                          in_=smx[0:1, :], func=AF.Square,
                          waits=pw + [("dve", dm)], inc=("act", 1), tag="ln_ms")
                d3 = P.op("dve", "tensor_sub", tev[0:1, 0:1024], tev[0:1, 0:1024],
                          tev[0:1, 1024:2048], waits=[("act", a2)], inc=("dve", 1),
                          tag="ln_v")
                a3 = P.op("act", "activation", out=tev[0:1, 1024:2048],
                          in_=tev[0:1, 0:1024], func=AF.Sqrt, bias=cstb[0:1, 0:1],
                          waits=[("dve", d3)], inc=("act", 1), tag="ln_s")
                d4 = P.op("dve", "reciprocal", srx[0:1, :], tev[0:1, 1024:2048],
                          waits=[("act", a3)], inc=("dve", 1), tag="ln_r")
                st["rdy"] = d4

            def bc(h):
                def f():
                    c = slice(h * 512, (h + 1) * 512)
                    pm = P.op("pe", "matmul", bank(4), onf[0:1, :], smx[0:1, c],
                              start=True, stop=True,
                              waits=[("dve", st["rdy"])] + st.get("bl", []),
                              inc=("pe", 1), tag="bcm")
                    P.op("dve", "tensor_copy", mbb[:, c], bank(4),
                         waits=[("pe", pm)], tag="bcmc")
                    pr = P.op("pe", "matmul", bank(5), onf[0:1, :], srx[0:1, c],
                              start=True, stop=True, waits=[("dve", st["rdy"])],
                              inc=("pe", 1), tag="bcr")
                    last = P.op("dve", "tensor_copy", rbb[:, c], bank(5),
                                waits=[("pe", pr)], inc=("dve", 1), tag="bcrc")
                    st["bl"] = [("dve", last)]
                    st["done"] = last
                return f

            return [dc_step(d) for d in range(NDC)] + [math, bc(0), bc(1)], st

        def queue_norm(tc, wfree, stt):
            t0 = tc * 1024

            def step(dc):
                def f():
                    s = dc % 2
                    lw = xl.load("pool", xfv[s],
                                 src[dc * 128:(dc + 1) * 128, t0:t0 + 1024],
                                 waits=[("dve", stt["done"])] + xshare.get(s, []))
                    P.op("dve", "tensor_sub", xfv[s][:, :], xfv[s][:, :], mbb[:, :],
                         waits=[lw] + ([("pe", wfree)] if dc == 0 else []), tag="f_sb")
                    post = P.op("dve", "tensor_mul", lnv[:, dc * 1024:(dc + 1) * 1024],
                                xfv[s][:, :], rbb[:, :], inc=("dve", 1), tag="f_ml")
                    xshare[s] = [("dve", post)]
                    ln_ready[tc] = post
                return f

            return [step(d) for d in range(NDC)]

        steps, stt = queue_stats(0)
        for f in steps:
            f()
        for f in queue_norm(0, 0, stt):
            f()

        for tc in range(4):
            t0 = tc * 1024
            last_ln = ln_ready[tc]
            wrel = {}
            p1 = p2 = None
            for oc in range(NHC):
                s = oc % 2
                lw1 = wl.load("sp", w1sl[s][:, :], w1s[oc, :, :],
                              waits=wrel.get(("1", s), []))
                lw2 = w2l.load("sp", w2sl[s][:, :], w2s[oc, :, :],
                               waits=wrel.get(("2", s), []))
                for hf in range(2):
                    j = oc * 2 + hf
                    b1i, b2i = j % 2, 2 + j % 2
                    b1, b2 = bank(b1i), bank(b2i)
                    for ic in range(NDC):
                        p1 = P.op("pe", "matmul", b1, w1sl[s][:, ic * 128:(ic + 1) * 128],
                                  lnv[:, ic * 1024 + hf * 512: ic * 1024 + hf * 512 + 512],
                                  start=(ic == 0), stop=(ic == NDC - 1),
                                  waits=[lw1, ("dve", last_ln)] + bank_rel.get(b1i, [])
                                  if ic == 0 else (),
                                  inc=("pe", 1) if ic == NDC - 1 else None, tag="f_w1")
                    sip = P.op("act", "activation", out=siv[hf][:, :], in_=b1,
                               func=AF.Silu, scale=s1b[:, oc:oc + 1],
                               waits=[("pe", p1)] + bank_rel.get(("si", hf), []),
                               inc=("act", 1), tag="f_si")
                    bank_rel[b1i] = [("act", sip)]
                    for ic in range(NDC):
                        p2 = P.op("pe", "matmul", b2, w2sl[s][:, ic * 128:(ic + 1) * 128],
                                  lnv[:, ic * 1024 + hf * 512: ic * 1024 + hf * 512 + 512],
                                  start=(ic == 0), stop=(ic == NDC - 1),
                                  waits=[lw2] + bank_rel.get(b2i, []) if ic == 0 else (),
                                  inc=("pe", 1) if ic == NDC - 1 else None, tag="f_w2")
                    gp = P.op("dve", "scalar_tensor_tensor",
                              a_sb[:, oc * 1024 + hf * 512: oc * 1024 + hf * 512 + 512],
                              b2, s2b[:, oc:oc + 1], siv[hf][:, :], ALU.mult, ALU.mult,
                              waits=[("pe", p2), ("act", sip)], inc=("dve", 1),
                              tag="f_ga")
                    bank_rel[b2i] = [("dve", gp)]
                    bank_rel[("si", hf)] = [("dve", gp)]
                wrel[("1", s)] = [("pe", p1)]
                wrel[("2", s)] = [("pe", p2)]
            a_done = ("dve", P.cnt["dve"])
            w2_free = p2

            if tc + 1 < 4:
                steps, stt = queue_stats(tc + 1)
                bg.extend(steps)
                norm_steps = queue_norm(tc + 1, w2_free, stt)
            else:
                norm_steps = []

            def h_layer(wsrc, rsrc, wdst, sv, ready):
                wrel2 = {}
                gp = None
                p1 = None
                for oc in range(NHC):
                    s = oc % 2
                    lw1 = wl.load("sp", hsl[s][:, :], wsrc[oc, :, :],
                                  waits=wrel2.get(s, [ready] if oc < 2 else []))
                    for hf in range(2):
                        j = oc * 2 + hf
                        bi = j % 4
                        b1 = bank(bi)
                        for ic in range(NHC):
                            p1 = P.op("pe", "matmul", b1,
                                      hsl[s][:, ic * 128:(ic + 1) * 128],
                                      rsrc[:, ic * 1024 + hf * 512: ic * 1024 + hf * 512 + 512],
                                      start=(ic == 0), stop=(ic == NHC - 1),
                                      waits=[lw1, ready] + bank_rel.get(bi, [])
                                      if ic == 0 else (),
                                      inc=("pe", 1) if ic == NHC - 1 else None,
                                      tag="f_h")
                        gp = P.op("act", "activation",
                                  out=wdst[:, oc * 1024 + hf * 512: oc * 1024 + hf * 512 + 512],
                                  in_=b1, func=AF.Gelu, scale=sv[:, oc:oc + 1],
                                  waits=[("pe", p1)], inc=("act", 1), tag="f_ge")
                        bank_rel[bi] = [("act", gp)]
                    wrel2[s] = [("pe", p1)]
                    drain(1)
                return ("act", gp)

            g1_done = h_layer(h1s, a_sb, g1, sh1b, a_done)
            bg.extend(norm_steps)
            g2_done = h_layer(h2s, g1, a_sb, sh2b, g1_done)
            drain(99)

            ot_rel = {}
            wrel3 = {}
            for oc in range(NDC):
                s = oc % 2
                lw1 = wl.load("sp", hsl[s][:, :], w3s[oc, :, :],
                              waits=wrel3.get(s, [g2_done] if oc < 2 else []))
                lw = xq.load("pool", xfv[s][:, :],
                             src[oc * 128:(oc + 1) * 128, t0:t0 + 1024],
                             waits=xshare.get(s, [a_done]))
                dp = None
                p1 = None
                for hf in range(2):
                    j = oc * 2 + hf
                    bi = j % 4
                    b1 = bank(bi)
                    for ic in range(NHC):
                        p1 = P.op("pe", "matmul", b1, hsl[s][:, ic * 128:(ic + 1) * 128],
                                  a_sb[:, ic * 1024 + hf * 512: ic * 1024 + hf * 512 + 512],
                                  start=(ic == 0), stop=(ic == NHC - 1),
                                  waits=[lw1, g2_done] + bank_rel.get(bi, [])
                                  if ic == 0 else (),
                                  inc=("pe", 1) if ic == NHC - 1 else None, tag="f_w3")
                    dp = P.op("dve", "scalar_tensor_tensor", otf[hf][:, :], b1,
                              s3b[:, oc:oc + 1], xfv[s][:, hf * 512:hf * 512 + 512],
                              ALU.mult, ALU.add,
                              waits=[("pe", p1), lw] + ot_rel.get(hf, []),
                              inc=("dve", 1), tag="f_e")
                    bank_rel[bi] = [("dve", dp)]
                    sw = st.load("pool",
                                 dst[oc * 128:(oc + 1) * 128,
                                     t0 + hf * 512:t0 + hf * 512 + 512],
                                 otf[hf][:, :], waits=[("dve", dp)])
                    ot_rel[hf] = [sw]
                xshare[s] = [("dve", dp)]
                wrel3[s] = [("pe", p1)]
        P.barrier()

    def emit_attn_b(src, dst, qkv_only=False):
        import collections
        lnb2 = [A.view(s * 8 * 1024, 128, NDC * 512, BF16) for s in range(2)]
        wqr = A.view(16 * 1024, 128, NDC * 1024, BF16)
        wkr = A.view(32 * 1024, 128, NDC * 1024, BF16)
        wvr = A.view(48 * 1024, 128, NDC * 1024, BF16)
        qsb = A.view(64 * 1024, 128, NDC * 512, BF16)
        ksb = A.view(72 * 1024, 128, NDC * 512, BF16)
        mbb = A.view(80 * 1024, 128, 512, F32)
        rbb = A.view(82 * 1024, 128, 512, F32)
        xfv = [A.view(84 * 1024 + s * 2048, 128, 512, F32) for s in range(2)]
        smx = A.view(88 * 1024, 1, 512, F32)
        srx = A.view(90 * 1024, 1, 512, F32)
        sxb = [A.view(92 * 1024 + s * 1024, 128, 512, BF16) for s in range(2)]
        sxqv = A.view(94 * 1024, 128, 512, BF16)
        tev = A.view(96 * 1024, 1, 1024, F32)
        vrow = [A.view(100 * 1024 + s * 2048, 128, 1024, BF16) for s in range(2)]

        P.sem("bw")
        for ic in range(NDC):
            sl = slice(ic * 128, (ic + 1) * 128)
            P.op("sp", "dma_start", out=wqr[:, ic * 1024:(ic + 1) * 1024],
                 in_=wqbs[sl, :], inc=("bw", 16), tag="bwq")
            P.op("sp", "dma_start", out=wkr[:, ic * 1024:(ic + 1) * 1024],
                 in_=wkbs[sl, :], inc=("bw", 16), tag="bwk")
            P.op("sp", "dma_start", out=wvr[:, ic * 1024:(ic + 1) * 1024],
                 in_=wvbs[sl, :], inc=("bw", 16), tag="bwv")
        BW = [("bw", P.cnt["bw"])]

        bg = collections.deque()

        def drain(n):
            for _ in range(min(n, len(bg))):
                bg.popleft()()

        lrel = {}
        xshare = {}
        ln_ready = {}
        pv_guard = {}

        def queue_stats4(tc):
            t0 = tc * 512
            st = {}

            def dc_step(dc):
                def f():
                    s = dc % 2
                    lw = xs.load("pool", xfv[s],
                                 src[dc * 128:(dc + 1) * 128, t0:t0 + 512],
                                 waits=xshare.get(s, []))
                    ap = P.op("act", "activation", out=sxqv, in_=xfv[s],
                              func=AF.Square, waits=[lw] + lrel.get("qq", []),
                              inc=("act", 1), tag="lnsq")
                    dp = P.op("dve", "tensor_copy", sxb[s], xfv[s],
                              waits=[lw] + lrel.get(("b", s), []),
                              inc=("dve", 1), tag="lncp")
                    xshare[s] = [("act", ap), ("dve", dp)]
                    pb = P.op("pe", "matmul", bank(4)[0:1, :], onbf[:, :], sxb[s],
                              start=(dc == 0), stop=(dc == NDC - 1),
                              waits=[("dve", dp)], inc=("pe", 1), tag="ln_sx")
                    pq = P.op("pe", "matmul", bank(5)[0:1, :], onbf[:, :], sxqv,
                              start=(dc == 0), stop=(dc == NDC - 1),
                              waits=[("act", ap)], inc=("pe", 1), tag="ln_sq")
                    lrel[("b", s)] = [("pe", pb)]
                    lrel["qq"] = [("pe", pq)]
                    if dc == NDC - 1:
                        st["pelast"] = max(pb, pq)
                return f

            def math():
                pw = [("pe", st["pelast"])]
                dm = P.op("dve", "tensor_scalar_mul", smx[0:1, :], bank(4)[0:1, :],
                          1.0 / D, waits=pw, inc=("dve", 1), tag="ln_m")
                P.op("dve", "tensor_scalar_mul", tev[0:1, 0:512], bank(5)[0:1, :],
                     1.0 / D, tag="ln_e")
                a2 = P.op("act", "activation", out=tev[0:1, 512:1024],
                          in_=smx[0:1, :], func=AF.Square,
                          waits=pw + [("dve", dm)], inc=("act", 1), tag="ln_ms")
                d3 = P.op("dve", "tensor_sub", tev[0:1, 0:512], tev[0:1, 0:512],
                          tev[0:1, 512:1024], waits=[("act", a2)], inc=("dve", 1),
                          tag="ln_v")
                a3 = P.op("act", "activation", out=tev[0:1, 512:1024],
                          in_=tev[0:1, 0:512], func=AF.Sqrt, bias=cstb[0:1, 0:1],
                          waits=[("dve", d3)], inc=("act", 1), tag="ln_s")
                d4 = P.op("dve", "reciprocal", srx[0:1, :], tev[0:1, 512:1024],
                          waits=[("act", a3)], inc=("dve", 1), tag="ln_r")
                st["rdy"] = d4

            def bc():
                pm = P.op("pe", "matmul", bank(4), onf[0:1, :], smx[0:1, :],
                          start=True, stop=True,
                          waits=[("dve", st["rdy"])] + st.get("bl", []),
                          inc=("pe", 1), tag="bcm")
                P.op("dve", "tensor_copy", mbb[:, :], bank(4), waits=[("pe", pm)],
                     tag="bcmc")
                pr = P.op("pe", "matmul", bank(5), onf[0:1, :], srx[0:1, :],
                          start=True, stop=True, waits=[("dve", st["rdy"])],
                          inc=("pe", 1), tag="bcr")
                last = P.op("dve", "tensor_copy", rbb[:, :], bank(5),
                            waits=[("pe", pr)], inc=("dve", 1), tag="bcrc")
                st["done"] = last

            return [dc_step(d) for d in range(NDC)] + [math, bc], st

        def queue_norm4(tc, stt):
            t0 = tc * 512
            lnb = lnb2[tc % 2]

            def step(dc):
                def f():
                    s = dc % 2
                    w = [("dve", stt["done"])] + xshare.get(s, [])
                    if dc == 0 and (tc - 2) in pv_guard:
                        w.append(("pe", pv_guard[tc - 2]))
                    lw = xl.load("pool", xfv[s],
                                 src[dc * 128:(dc + 1) * 128, t0:t0 + 512], waits=w)
                    P.op("dve", "tensor_sub", xfv[s][:, :], xfv[s][:, :], mbb[:, :],
                         waits=[lw] + ([("pe", pv_guard[tc - 2])]
                                       if (dc == 0 and (tc - 2) in pv_guard) else []),
                         tag="b_sb")
                    post = P.op("dve", "tensor_mul", lnb[:, dc * 512:(dc + 1) * 512],
                                xfv[s][:, :], rbb[:, :], inc=("dve", 1), tag="b_ml")
                    xshare[s] = [("dve", post)]
                    ln_ready[tc] = post
                return f

            return [step(d) for d in range(NDC)]

        steps, stt = queue_stats4(0)
        for f in steps:
            f()
        for f in queue_norm4(0, stt):
            f()

        qrel = {}
        for tc in range(8):
            t0 = tc * 512
            lnb = lnb2[tc % 2]
            if tc not in ln_ready:
                drain(99)
            last_ln = ln_ready[tc]
            if tc + 1 < 8:
                steps, stt = queue_stats4(tc + 1)
                bg.extend(steps)
                bg.extend(queue_norm4(tc + 1, stt))
            qk_rel = {}
            dq = dk = None
            for oc in range(NDC):
                bq, bk = bank(oc % 2), bank(2 + oc % 2)
                pq = pk = None
                for ic in range(NDC):
                    pq = P.op("pe", "matmul", bq,
                              wqr[:, ic * 1024 + oc * 128: ic * 1024 + oc * 128 + 128],
                              lnb[:, ic * 512:(ic + 1) * 512],
                              start=(ic == 0), stop=(ic == NDC - 1),
                              waits=BW + [("dve", last_ln)] + qk_rel.get(("q", oc % 2), [])
                              if ic == 0 else (),
                              inc=("pe", 1) if ic == NDC - 1 else None, tag="b_q")
                dq = P.op("act", "activation", out=qsb[:, oc * 512:(oc + 1) * 512],
                          in_=bq, func=AF.Copy,
                          waits=[("pe", pq)] + (qrel.get("q", []) if oc == 0 else []),
                          inc=("act", 1), tag="b_qc")
                qk_rel[("q", oc % 2)] = [("act", dq)]
                for ic in range(NDC):
                    pk = P.op("pe", "matmul", bk,
                              wkr[:, ic * 1024 + oc * 128: ic * 1024 + oc * 128 + 128],
                              lnb[:, ic * 512:(ic + 1) * 512],
                              start=(ic == 0), stop=(ic == NDC - 1),
                              waits=qk_rel.get(("k", oc % 2), []) if ic == 0 else (),
                              inc=("pe", 1) if ic == NDC - 1 else None, tag="b_k")
                dk = P.op("dve", "tensor_copy", ksb[:, oc * 512:(oc + 1) * 512], bk,
                          waits=[("pe", pk)] + (qrel.get("k", []) if oc == 0 else []),
                          inc=("dve", 1), tag="b_kc")
                qk_rel[("k", oc % 2)] = [("dve", dk)]
                drain(2)
            qk_ev = ("dve", P.cnt["dve"])
            qk_ev_a = ("act", P.cnt["act"])
            sq = qst.load("pool", qS[:, :, t0:t0 + 512].rearrange("a p t -> p a t"),
                          qsb.rearrange("p (a t) -> p a t", t=512),
                          waits=[qk_ev, qk_ev_a])
            sk = qst.load("pool", kS[:, :, t0:t0 + 512].rearrange("a p t -> p a t"),
                          ksb.rearrange("p (a t) -> p a t", t=512), waits=[qk_ev])
            qrel["q"], qrel["k"] = [sq], [sk]
            pv = None
            for tb in range(4):
                s = tb % 2
                vdone = None
                for hf in range(2):
                    bv = bank(6 + (tb * 2 + hf) % 2)
                    wv = BW + [("dve", last_ln)] + \
                        qrel.get(("vb", (tb * 2 + hf) % 2), [])
                    for ic in range(NDC):
                        pv = P.op("pe", "matmul", bv,
                                  lnb[:, ic * 512 + tb * 128: ic * 512 + tb * 128 + 128],
                                  wvr[:, ic * 1024 + hf * 512: ic * 1024 + hf * 512 + 512],
                                  start=(ic == 0), stop=(ic == NDC - 1),
                                  waits=wv if ic == 0 else (),
                                  inc=("pe", 1) if ic == NDC - 1 else None, tag="b_v")
                    vdone = P.op("dve", "tensor_copy", vrow[s][:, hf * 512:hf * 512 + 512],
                                 bv, waits=[("pe", pv)] +
                                 (qrel.get(("v", s), []) if hf == 0 else []),
                                 inc=("dve", 1), tag="b_vc")
                    qrel[("vb", (tb * 2 + hf) % 2)] = [("dve", vdone)]
                sv = qst.load("pool", vS[t0 + tb * 128: t0 + tb * 128 + 128, :],
                              vrow[s][:, :], waits=[("dve", vdone)])
                qrel[("v", s)] = [sv]
                drain(1)
            pv_guard[tc] = pv
            drain(4)
        drain(99)
        P.barrier()
        if qkv_only:
            return

        strip = A.view(0, 128, 32 * 512, BF16)
        qtc = A.view(32 * 1024, 128, NDC * 512, BF16)
        kbf = [A.view(40 * 1024 + s * 8192, 128, 4096, BF16) for s in range(2)]
        vbf = [A.view(56 * 1024 + s * 2048, 128, 1024, BF16) for s in range(2)]
        wtb = [A.view(60 * 1024 + s * 2048, 128, 512, F32) for s in range(2)]
        wor = A.view(64 * 1024, 128, NDC * 1024, BF16)
        retrT = A.view(80 * 1024, 128, NDC * 512, BF16)
        ttf = [A.view(88 * 1024 + s * 2048, 128, 512, F32) for s in range(2)]
        otf = [A.view(92 * 1024 + s * 2048, 128, 512, F32) for s in range(2)]
        xf2 = [A.view(96 * 1024 + s * 2048, 128, 512, F32) for s in range(2)]
        tmpw = [A.view(100 * 1024 + s * 2048, 128, 512, F32) for s in range(2)]

        P.sem("bw2")
        for ic in range(NDC):
            P.op("sp", "dma_start", out=wor[:, ic * 1024:(ic + 1) * 1024],
                 in_=wobs[ic * 128:(ic + 1) * 128, :], inc=("bw2", 16), tag="bwo")
        BW2 = [("bw2", P.cnt["bw2"])]

        ot_rel = {}
        xr_rel = {}
        kb_rel = {}
        vb_rel = {}
        wt_rel = {}
        strip_rel = {}
        for tc in range(8):
            t0 = tc * 512
            sb0 = 4 * tc
            nsb = 32 - sb0
            lq = ql.load("sp", qtc.rearrange("p (a t) -> p a t", t=512),
                         qS[:, :, t0:t0 + 512].rearrange("a p t -> p a t"),
                         waits=[("pe", P.cnt["pe"])])
            strip_done = None
            sd_posts = {}
            for j4 in range((nsb + 3) // 4):
                sbs = list(range(sb0 + j4 * 4, min(sb0 + (j4 + 1) * 4, 32)))
                slen = len(sbs)
                s = j4 % 2
                lk = kl.load(
                    "sp",
                    kbf[s][:, 0:NDC * slen * 128].rearrange("p (a t) -> p a t", a=NDC),
                    kS[:, :, sbs[0] * 128: sbs[0] * 128 + slen * 128].rearrange(
                        "a p t -> p a t"),
                    waits=kb_rel.get(s, []))
                pk = None
                for jj, sb in enumerate(sbs):
                    pp = bank(sb % 2)
                    delta = sb * 128 - t0
                    for ic in range(NDC):
                        pk = P.op("pe", "matmul", pp,
                                  kbf[s][:, ic * slen * 128 + jj * 128:
                                         ic * slen * 128 + jj * 128 + 128],
                                  qtc[:, ic * 512:(ic + 1) * 512],
                                  start=(ic == 0), stop=(ic == NDC - 1),
                                  waits=[lq, lk] + strip_rel.get(sb % 2, [])
                                  if ic == 0 else (),
                                  inc=("pe", 1) if ic == NDC - 1 else None, tag="b_s")
                    ws = sb % 2
                    if delta >= 512:
                        wp = P.op("act", "activation", out=wtb[ws][:, :], in_=cio[:, :],
                                  func=AF.Exp, scale=lndb[:, 0:1],
                                  bias=ldt[:, delta // 128:delta // 128 + 1],
                                  waits=wt_rel.get(ws, []), inc=("act", 1), tag="b_w")
                        wpd = ("act", wp)
                    else:
                        P.op("dve", "tensor_scalar", tmpw[ws][:, :], cio[:, :],
                             float(delta - 1), lndb[:, 0:1], ALU.add, ALU.mult,
                             waits=wt_rel.get(ws, []), tag="b_w1")
                        dw2 = P.op("dve", "tensor_scalar_min", tmpw[ws][:, :],
                                   tmpw[ws][:, :], 0.0, inc=("dve", 1), tag="b_w2")
                        wp = P.op("act", "activation", out=wtb[ws][:, :],
                                  in_=tmpw[ws][:, :], func=AF.Exp,
                                  waits=[("dve", dw2)] + wt_rel.get(ws, []),
                                  inc=("act", 1), tag="b_w3")
                        c0 = 384 - delta
                        wpm = P.op("dve", "tensor_mul", wtb[ws][:, :], wtb[ws][:, :],
                                   t1m[:, c0:c0 + 512], waits=[("act", wp)],
                                   inc=("dve", 1), tag="b_w4")
                        wpd = ("dve", wpm)
                    sd = P.op("dve", "tensor_mul",
                              strip[:, (sb - sb0) * 512:(sb - sb0) * 512 + 512],
                              pp, wtb[ws][:, :], waits=[("pe", pk), wpd],
                              inc=("dve", 1), tag="b_p")
                    sd_posts[sb - sb0] = [("dve", sd)]
                    wt_rel[ws] = [("dve", sd)]
                    strip_rel[sb % 2] = [("dve", sd)]
                    strip_done = sd
                kb_rel[s] = [("pe", pk)]
            retr_stop = {}
            for jj in range(nsb):
                sb = sb0 + jj
                s = jj % 2
                lv = vl.load("sp", vbf[s][:, :], vS[sb * 128:(sb + 1) * 128, :],
                             waits=vb_rel.get(s, []))
                pr = None
                for oc in range(2, NDC):
                    pr = P.op("pe", "matmul", bank(oc), vbf[s][:, oc * 128:(oc + 1) * 128],
                              strip[:, jj * 512:jj * 512 + 512],
                              start=(jj == 0), stop=(jj == nsb - 1),
                              waits=sd_posts[jj] + [lv] if oc == 2 else (),
                              inc=("pe", 1) if oc == NDC - 1 else None, tag="b_r")
                    if jj == nsb - 1:
                        retr_stop[oc] = pr
                vb_rel[s] = [("pe", pr)]
            rv = None
            for oc in range(2, NDC):
                rv = P.op("dve", "tensor_copy", retrT[:, oc * 512:(oc + 1) * 512],
                          bank(oc), waits=[("pe", retr_stop[NDC - 1])],
                          inc=("dve", 1), tag="b_rc")
            for jj in range(nsb):
                sb = sb0 + jj
                s = jj % 2
                lv = vl.load("sp", vbf[s][:, :], vS[sb * 128:(sb + 1) * 128, :],
                             waits=vb_rel.get(s, []))
                pr = None
                for oc in range(2):
                    pr = P.op("pe", "matmul", bank(oc), vbf[s][:, oc * 128:(oc + 1) * 128],
                              strip[:, jj * 512:jj * 512 + 512],
                              start=(jj == 0), stop=(jj == nsb - 1),
                              waits=[("dve", strip_done), lv] if oc == 0 else (),
                              inc=("pe", 1) if oc == 1 else None, tag="b_r2")
                    if jj == nsb - 1:
                        retr_stop[oc] = pr
                vb_rel[s] = [("pe", pr)]
            for oc in range(2):
                rv = P.op("dve", "tensor_copy", retrT[:, oc * 512:(oc + 1) * 512],
                          bank(oc), waits=[("pe", retr_stop[1])],
                          inc=("dve", 1), tag="b_rc2")
            retr_done = rv
            ev = {}
            for ocO in range(NDC):
                s = ocO % 2
                bnk = bank(ocO % 4)
                lw = xq.load("pool", xf2[s][:, :],
                             src[ocO * 128:(ocO + 1) * 128, t0:t0 + 512],
                             waits=xr_rel.get(s, []))
                p1 = None
                for ic in range(NDC):
                    p1 = P.op("pe", "matmul", bnk,
                              wor[:, ic * 1024 + ocO * 128: ic * 1024 + ocO * 128 + 128],
                              retrT[:, ic * 512:(ic + 1) * 512],
                              start=(ic == 0), stop=(ic == NDC - 1),
                              waits=BW2 + [("dve", retr_done)] + ev.get(ocO % 4, [])
                              if ic == 0 else (),
                              inc=("pe", 1) if ic == NDC - 1 else None, tag="b_o")
                dp = P.op("dve", "tensor_add", otf[s][:, :], bnk, xf2[s][:, :],
                          waits=[("pe", p1), lw] + ot_rel.get(s, []),
                          inc=("dve", 1), tag="b_e")
                ev[ocO % 4] = [("dve", dp)]
                sw = st.load("pool", dst[ocO * 128:(ocO + 1) * 128, t0:t0 + 512],
                             otf[s][:, :], waits=[("dve", dp)])
                ot_rel[s] = [sw]
                xr_rel[s] = [("dve", dp)]
            P.after("pe", [("act", P.cnt["act"]), ("dve", P.cnt["dve"])])
        P.barrier()

    if nsub == 1:
        emit_attn_a(xin, yout)
    elif nsub == 2:
        emit_attn_a(xin, xr1)
        emit_ffn(xr1, yout)
    elif nsub == 30:
        emit_attn_a(xin, xr1)
        emit_ffn(xr1, xr2)
        emit_attn_b(xr2, yout, qkv_only=True)
    elif nsub == 3:
        emit_attn_a(xin, xr1)
        emit_ffn(xr1, xr2)
        emit_attn_b(xr2, yout)
    else:
        emit_attn_a(xin, xr1)
        emit_ffn(xr1, xr2)
        emit_attn_b(xr2, xr3)
        emit_ffn(xr3, yout)

    fin = [(s, P.cnt[s]) for s in P.cnt if s.startswith("st") and P.cnt[s] > 0]
    P.sem("fz")
    P.op("pool", "dma_start", out=xr1[0:1, 0:1], in_=yout[0:1, 0:1],
         waits=fin, inc=("fz", 16), tag="fin")

    from contextlib import ExitStack
    with ExitStack() as es:
        sems = {name: es.enter_context(nc.semaphore(name)) for name in P.cnt}
        block = es.enter_context(nc.Block())
        block.sync(lambda e: P.replay("sp", e, sems))
        block.scalar(lambda e: P.replay("act", e, sems))
        block.vector(lambda e: P.replay("dve", e, sems))
        block.tensor(lambda e: P.replay("pe", e, sems))
        block.gpsimd(lambda e: P.replay("pool", e, sems))
    return nc



def _tile_oc(WT, nic, noc):
    return np.ascontiguousarray(
        WT.reshape(nic, 128, noc, 128).transpose(2, 1, 0, 3).reshape(noc, 128, nic * 128))


def _prep_common(inp):
    f32 = np.float32

    def bd(w):
        out = np.zeros((128, 128), f32)
        wt = np.asarray(w, f32).T
        out[0:64, 0:64] = wt
        out[64:128, 64:128] = wt
        return out

    p = np.arange(128)[:, None]
    c = np.arange(896)[None, :]
    return {
        "wqa": bd(inp["L0_attn_a_Wq"]),
        "wka": bd(inp["L0_attn_a_Wk"]),
        "wva": bd(inp["L0_attn_a_Wv"]),
        "wut": np.ascontiguousarray(np.asarray(inp["L0_attn_a_Wu"], f32).T),
        "but": np.ascontiguousarray(np.asarray(inp["L0_attn_a_bu"], f32).reshape(NDC, 128).T),
        "wqbt": np.ascontiguousarray(np.asarray(inp["L0_attn_b_Wq"], f32).T),
        "wkbt": np.ascontiguousarray(np.asarray(inp["L0_attn_b_Wk"], f32).T),
        "wvbt": np.ascontiguousarray(np.asarray(inp["L0_attn_b_Wv"], f32).T),
        "wobt": np.ascontiguousarray(np.asarray(inp["L0_attn_b_Wo"], f32).T),
        "dlog": np.asarray(inp["L0_attn_b_decay_logit"], f32).reshape(1, 1),
        "oscl": np.asarray(inp["L0_attn_b_out_scale"], f32).reshape(1, 1),
        "w1t": _tile_oc(np.asarray(inp["L0_ffn_w1_w"], f32).T, NDC, NHC),
        "w2t": _tile_oc(np.asarray(inp["L0_ffn_w2_w"], f32).T, NDC, NHC),
        "h1t": _tile_oc(np.asarray(inp["L0_ffn_h1_w"], f32).T, NHC, NHC),
        "h2t": _tile_oc(np.asarray(inp["L0_ffn_h2_w"], f32).T, NHC, NHC),
        "w3t": _tile_oc(np.asarray(inp["L0_ffn_w3_w"], f32).T, NHC, NDC),
        "s1v": np.ascontiguousarray(np.asarray(inp["L0_ffn_w1_s"], f32).reshape(NHC, 128).T),
        "s2v": np.ascontiguousarray(np.asarray(inp["L0_ffn_w2_s"], f32).reshape(NHC, 128).T),
        "sh1v": np.ascontiguousarray(np.asarray(inp["L0_ffn_h1_s"], f32).reshape(NHC, 128).T),
        "sh2v": np.ascontiguousarray(np.asarray(inp["L0_ffn_h2_s"], f32).reshape(NHC, 128).T),
        "s3v": np.ascontiguousarray(np.asarray(inp["L0_ffn_w3_s"], f32).reshape(NDC, 128).T),
        "m1c": (p > c - 384).astype(f32),
        "ioc": (np.arange(128)[:, None] - np.arange(512)[None, :]).astype(f32),
        "dm1": (128.0 * np.arange(32)[None, :] - 1.0).astype(f32),
        "on1": np.ones((1, 128), f32),
        "cstv": np.tile(np.array([[EPS, EXP_SHIFT, SCALE_A, SCALE_B]], f32), (128, 1)),
    }


def kernel(**inp):
    x = np.asarray(inp["L0_attn_a_x"], np.float32)
    xT = np.ascontiguousarray(x.transpose(0, 2, 1))
    common = _prep_common(inp)
    nc = build_program(NSUB)
    in_maps = [dict(common, xT=np.ascontiguousarray(xT[b])) for b in range(NCORES)]
    res = run_bass_kernel_spmd(nc, in_maps, core_ids=list(range(NCORES)))
    y = np.stack([np.asarray(r["y"]) for r in res.results])
    return np.ascontiguousarray(y.transpose(0, 2, 1)).astype(np.float32)


# version 05
# speedup vs baseline: 1.0874x; 1.0874x over previous
import math
import numpy as np
import concourse.bass as bass
import concourse.mybir as mybir
from concourse.bass_utils import run_bass_kernel_spmd

F32 = mybir.dt.float32
BF16 = mybir.dt.bfloat16
U8 = mybir.dt.uint8
AF = mybir.ActivationFunctionType
ALU = mybir.AluOpType

NCORES = 8
T = 4096
D = 1024
NDC = D // 128
DH = 4096
NHC = DH // 128
SCALE_A = float(D) ** -0.25
SCALE_B = 1.0 / math.sqrt(float(D))
EPS = 1e-5
EXP_SHIFT = -20.0

NSUB = 4

ENGS = ("sp", "act", "dve", "pe", "pool")


class Prog:

    def __init__(self):
        self.ops = {e: [] for e in ENGS}
        self.cnt = {}
        self.pending = {e: {} for e in ENGS}

    def sem(self, name):
        assert name not in self.cnt
        self.cnt[name] = 0

    def op(self, eng, method, *args, waits=(), inc=None, tag="", **kwargs):
        w = dict(self.pending[eng])
        self.pending[eng] = {}
        for s, v in waits:
            assert v <= self.cnt[s], (tag, s, v, self.cnt[s])
            if w.get(s, -1) < v:
                w[s] = v
        post = None
        if inc is not None:
            s, n = inc
            self.cnt[s] += n
            post = self.cnt[s]
        self.ops[eng].append((tuple(w.items()), method, args, kwargs, inc, tag))
        return post

    def after(self, eng, waits):
        p = self.pending[eng]
        for s, v in waits:
            if p.get(s, -1) < v:
                p[s] = v

    def barrier(self):
        snap = {s: v for s, v in self.cnt.items() if v > 0}
        for e in ENGS:
            p = self.pending[e]
            for s, v in snap.items():
                if p.get(s, -1) < v:
                    p[s] = v

    def replay(self, eng, proxy, sems):
        for waits, method, args, kwargs, inc, tag in self.ops[eng]:
            for s, v in waits:
                proxy.wait_ge(sems[s], v)
            inst = getattr(proxy, method)(*args, **kwargs)
            if inc is not None:
                assert inst is not None, f"op {tag} returned no instruction"
                inst.then_inc(sems[inc[0]], inc[1])


class Arena:
    def __init__(self, nc, pbytes):
        self.h = nc.alloc_sbuf_tensor("arena", [128, pbytes], U8)
        self.pbytes = pbytes
        self._v = {}

    def view(self, off, npart, cols, dtype):
        isz = mybir.dt.size(dtype)
        assert off % 64 == 0 and off + cols * isz <= self.pbytes, (off, cols, isz)
        key = (off, npart, cols, str(dtype))
        if key not in self._v:
            self._v[key] = self.h.bitcast(dtype).ap()[0:npart, off // isz: off // isz + cols]
        return self._v[key]


class DBuf:

    def __init__(self, P, name, nslots=2):
        self.P, self.name, self.n = P, name, nslots
        self.ld = [f"{name}{i}" for i in range(nslots)]
        for s in self.ld:
            P.sem(s)
        self.issued = [0] * nslots
        self.i = 0

    def load(self, qeng, dst, src, waits=()):
        s = self.i % self.n
        self.i += 1
        post = self.P.op(qeng, "dma_start", out=dst, in_=src, waits=waits,
                         inc=(self.ld[s], 16), tag=self.name)
        self.issued[s] += 1
        assert post == 16 * self.issued[s]
        return (self.ld[s], post)


def build_program(nsub=NSUB):
    nc = bass.Bass("TRN2", target_bir_lowering=False)
    P = Prog()
    for s in ("pe", "act", "dve"):
        P.sem(s)

    xin = nc.dram_tensor("xT", [D, T], F32, kind="ExternalInput").ap()
    wqa = nc.dram_tensor("wqa", [128, 128], F32, kind="ExternalInput").ap()
    wka = nc.dram_tensor("wka", [128, 128], F32, kind="ExternalInput").ap()
    wva = nc.dram_tensor("wva", [128, 128], F32, kind="ExternalInput").ap()
    wut = nc.dram_tensor("wut", [D, D], F32, kind="ExternalInput").ap()
    but = nc.dram_tensor("but", [128, NDC], F32, kind="ExternalInput").ap()
    wqbt = nc.dram_tensor("wqbt", [D, D], F32, kind="ExternalInput").ap()
    wkbt = nc.dram_tensor("wkbt", [D, D], F32, kind="ExternalInput").ap()
    wvbt = nc.dram_tensor("wvbt", [D, D], F32, kind="ExternalInput").ap()
    wobt = nc.dram_tensor("wobt", [D, D], F32, kind="ExternalInput").ap()
    dlog = nc.dram_tensor("dlog", [1, 1], F32, kind="ExternalInput").ap()
    oscl = nc.dram_tensor("oscl", [1, 1], F32, kind="ExternalInput").ap()
    w1t = nc.dram_tensor("w1t", [NHC, 128, NDC * 128], F32, kind="ExternalInput").ap()
    w2t = nc.dram_tensor("w2t", [NHC, 128, NDC * 128], F32, kind="ExternalInput").ap()
    h1t = nc.dram_tensor("h1t", [NHC, 128, NHC * 128], F32, kind="ExternalInput").ap()
    h2t = nc.dram_tensor("h2t", [NHC, 128, NHC * 128], F32, kind="ExternalInput").ap()
    w3t = nc.dram_tensor("w3t", [NDC, 128, NHC * 128], F32, kind="ExternalInput").ap()
    s1v = nc.dram_tensor("s1v", [128, NHC], F32, kind="ExternalInput").ap()
    s2v = nc.dram_tensor("s2v", [128, NHC], F32, kind="ExternalInput").ap()
    sh1v = nc.dram_tensor("sh1v", [128, NHC], F32, kind="ExternalInput").ap()
    sh2v = nc.dram_tensor("sh2v", [128, NHC], F32, kind="ExternalInput").ap()
    s3v = nc.dram_tensor("s3v", [128, NDC], F32, kind="ExternalInput").ap()
    m1c = nc.dram_tensor("m1c", [128, 896], F32, kind="ExternalInput").ap()
    ioc = nc.dram_tensor("ioc", [128, 512], F32, kind="ExternalInput").ap()
    dm1 = nc.dram_tensor("dm1", [1, 32], F32, kind="ExternalInput").ap()
    on1 = nc.dram_tensor("on1", [1, 128], F32, kind="ExternalInput").ap()
    cstv = nc.dram_tensor("cstv", [128, 4], F32, kind="ExternalInput").ap()
    yout = nc.dram_tensor("y", [D, T], F32, kind="ExternalOutput").ap()

    xr1 = nc.dram_tensor("xr1", [D, T], F32, kind="Internal").ap()
    xr2 = nc.dram_tensor("xr2", [D, T], F32, kind="Internal").ap()
    xr3 = nc.dram_tensor("xr3", [D, T], F32, kind="Internal").ap()
    w1s = nc.dram_tensor("w1s", [NHC, 128, NDC * 128], BF16, kind="Internal").ap()
    w2s = nc.dram_tensor("w2s", [NHC, 128, NDC * 128], BF16, kind="Internal").ap()
    h1s = nc.dram_tensor("h1s", [NHC, 128, NHC * 128], BF16, kind="Internal").ap()
    h2s = nc.dram_tensor("h2s", [NHC, 128, NHC * 128], BF16, kind="Internal").ap()
    w3s = nc.dram_tensor("w3s", [NDC, 128, NHC * 128], BF16, kind="Internal").ap()
    wqbs = nc.dram_tensor("wqbs", [D, D], BF16, kind="Internal").ap()
    wkbs = nc.dram_tensor("wkbs", [D, D], BF16, kind="Internal").ap()
    wvbs = nc.dram_tensor("wvbs", [D, D], BF16, kind="Internal").ap()
    wobs = nc.dram_tensor("wobs", [D, D], BF16, kind="Internal").ap()
    qS = nc.dram_tensor("qS", [NDC, 128, T], BF16, kind="Internal").ap()
    kS = nc.dram_tensor("kS", [NDC, 128, T], BF16, kind="Internal").ap()
    vS = nc.dram_tensor("vS", [T, D], BF16, kind="Internal").ap()

    def sbuf(name, p, cols, dtype):
        return nc.alloc_sbuf_tensor(name, [p, cols], dtype).ap()

    t1m = sbuf("t1m", 128, 896, F32)
    t0m = sbuf("t0m", 128, 896, BF16)
    cio = sbuf("cio", 128, 512, F32)
    ldt = sbuf("ldt", 128, 32, F32)
    lndb = sbuf("lndb", 128, 1, F32)
    oscb = sbuf("oscb", 128, 1, F32)
    onbf = sbuf("onbf", 128, 1, BF16)
    onf = sbuf("onf", 1, 128, F32)
    busb = sbuf("busb", 128, NDC, F32)
    s1b = sbuf("s1b", 128, NHC, F32)
    s2b = sbuf("s2b", 128, NHC, F32)
    sh1b = sbuf("sh1b", 128, NHC, F32)
    sh2b = sbuf("sh2b", 128, NHC, F32)
    s3b = sbuf("s3b", 128, NDC, F32)
    wqab = sbuf("wqab", 128, 128, BF16)
    wkab = sbuf("wkab", 128, 128, BF16)
    wvab = sbuf("wvab", 128, 128, BF16)
    sc0 = sbuf("sc0", 1, 8, F32)
    dm1s = sbuf("dm1s", 1, 32, F32)
    wstg = sbuf("wstg", 128, 3 * 128, F32)
    cstb = sbuf("cstb", 128, 4, F32)

    ARENA = 196 * 1024
    A = Arena(nc, ARENA)
    ps = [nc.alloc_psum_tensor(f"ps{i}", [128, 1024], F32).ap() for i in range(4)]

    def bank(i):
        return ps[i // 2][:, (i % 2) * 512:(i % 2) * 512 + 512]

    def pair(i):
        return ps[i]

    cl = DBuf(P, "cl", 2)
    cs = DBuf(P, "cs", 2)
    xl = DBuf(P, "xl", 2)
    xs = DBuf(P, "xs", 2)
    xq = DBuf(P, "xq", 2)
    wl = DBuf(P, "wl", 2)
    w2l = DBuf(P, "w2l", 2)
    kl = DBuf(P, "kl", 2)
    vl = DBuf(P, "vl", 2)
    ql = DBuf(P, "ql", 1)
    st = DBuf(P, "st", 2)
    qst = DBuf(P, "qst", 2)

    P.sem("c0")
    for dst_ap, src_ap in (
        (t1m[:, :], m1c[:, :]), (cio[:, :], ioc[:, :]), (onf[:, :], on1[:, :]),
        (busb[:, :], but[:, :]), (s1b[:, :], s1v[:, :]), (s2b[:, :], s2v[:, :]),
        (sh1b[:, :], sh1v[:, :]), (sh2b[:, :], sh2v[:, :]), (s3b[:, :], s3v[:, :]),
        (sc0[0:1, 0:1], dlog[:, :]), (sc0[0:1, 1:2], oscl[:, :]),
        (dm1s[0:1, :], dm1[:, :]), (wstg[:, 0:128], wqa[:, :]),
        (wstg[:, 128:256], wka[:, :]), (wstg[:, 256:384], wva[:, :]),
        (cstb[:, :], cstv[:, :]),
    ):
        P.op("sp", "dma_start", out=dst_ap, in_=src_ap, inc=("c0", 16), tag="c0")
    CV = [("c0", P.cnt["c0"])]

    P.op("dve", "memset", onbf[:, :], 1.0, tag="onbf")
    P.op("dve", "tensor_scalar", t0m[:, :], t1m[:, :], -1.0, 1.0, ALU.mult, ALU.add,
         waits=CV, tag="t0m")
    a0 = P.op("act", "activation", out=sc0[0:1, 2:3], in_=sc0[0:1, 0:1],
              func=AF.Sigmoid, waits=CV, inc=("act", 1), tag="sig")
    d0 = P.op("dve", "tensor_scalar_max", sc0[0:1, 3:4], sc0[0:1, 2:3], 1e-38,
              waits=[("act", a0)], inc=("dve", 1), tag="sigm")
    a1 = P.op("act", "activation", out=sc0[0:1, 4:5], in_=sc0[0:1, 3:4], func=AF.Ln,
              waits=[("dve", d0)], inc=("act", 1), tag="lnd")
    p1 = P.op("pe", "matmul", bank(0)[:, 0:1], onf[0:1, :], sc0[0:1, 4:5],
              start=True, stop=True, waits=[("act", a1)], inc=("pe", 1), tag="bc1")
    p2 = P.op("pe", "matmul", bank(1)[:, 0:1], onf[0:1, :], sc0[0:1, 1:2],
              start=True, stop=True, waits=CV, inc=("pe", 1), tag="bc2")
    p3 = P.op("pe", "matmul", bank(2)[:, 0:32], onf[0:1, :], dm1s[0:1, :],
              start=True, stop=True, waits=CV, inc=("pe", 1), tag="bc3")
    P.op("dve", "tensor_copy", lndb[:, :], bank(0)[:, 0:1], waits=[("pe", p1)],
         tag="lndc")
    P.op("dve", "tensor_copy", oscb[:, :], bank(1)[:, 0:1], waits=[("pe", p2)],
         tag="oscc")
    d3 = P.op("dve", "tensor_scalar", ldt[:, :], bank(2)[:, 0:32], lndb[:, 0:1], None,
              ALU.mult, waits=[("pe", p3)], inc=("dve", 1), tag="ldt")
    oscb_ready = d3
    P.op("dve", "tensor_scalar_mul", wqab[:, :], wstg[:, 0:128], SCALE_A,
         waits=CV, tag="wqab")
    P.op("dve", "tensor_scalar_mul", wkab[:, :], wstg[:, 128:256], SCALE_A,
         tag="wkab")
    P.op("act", "activation", out=wvab[:, :], in_=wstg[:, 256:384], func=AF.Copy,
         waits=CV, inc=("act", 1), tag="wvab")

    wu_sb = A.view(0, 128, NDC * 1024, BF16)
    CF, CB = 16 * 1024, 48 * 1024
    cl_rel = {}
    st_rel = {}

    def conv(src_ap, dst_ap, cols, scale=None, scale_ap=None, to_sbuf=None,
             extra_waits=()):
        s = cl.i % 2
        assert cols * 4 <= 16 * 1024
        cf = A.view(CF + s * 16 * 1024, 128, cols, F32)
        lw = cl.load("sp", cf, src_ap,
                     waits=list(extra_waits) + cl_rel.get(s, []))
        scl = scale_ap if scale_ap is not None else scale
        out_ap = to_sbuf if to_sbuf is not None else A.view(CB + s * 8 * 1024, 128,
                                                            cols, BF16)
        wst = ([] if to_sbuf is not None else st_rel.get(s, []))
        if scl is None:
            post = P.op("act", "activation", out=out_ap, in_=cf, func=AF.Copy,
                        waits=[lw] + wst, inc=("act", 1), tag="cv")
            cl_rel[s] = [("act", post)]
        else:
            post = P.op("dve", "tensor_scalar_mul", out_ap, cf, scl,
                        waits=[lw] + wst, inc=("dve", 1), tag="cvs")
            cl_rel[s] = [("dve", post)]
        if to_sbuf is None:
            sw = cs.load("pool", dst_ap, out_ap, waits=[cl_rel[s][0]])
            st_rel[s] = [sw]

    for ic in range(NDC):
        conv(wut[ic * 128:(ic + 1) * 128, :], None, D,
             to_sbuf=wu_sb[:, ic * 1024:(ic + 1) * 1024])
    P.barrier()

    pieces = []
    if nsub >= 2:
        for oc in range(NHC):
            pieces.append((w1t[oc, :, :], w1s[oc, :, :], None, None))
            pieces.append((w2t[oc, :, :], w2s[oc, :, :], None, None))
        for oc in range(NHC):
            for qq in range(4):
                c = slice(qq * 1024, (qq + 1) * 1024)
                pieces.append((h1t[oc, :, c], h1s[oc, :, c], None, None))
                pieces.append((h2t[oc, :, c], h2s[oc, :, c], None, None))
        for oc in range(NDC):
            for qq in range(4):
                c = slice(qq * 1024, (qq + 1) * 1024)
                pieces.append((w3t[oc, :, c], w3s[oc, :, c], None, None))
    if nsub >= 3:
        for ic in range(NDC):
            sl = slice(ic * 128, (ic + 1) * 128)
            pieces.append((wqbt[sl, :], wqbs[sl, :], cstb[:, 3:4], None))
            pieces.append((wkbt[sl, :], wkbs[sl, :], None, None))
            pieces.append((wvbt[sl, :], wvbs[sl, :], None, None))
            pieces.append((wobt[sl, :], wobs[sl, :], oscb[:, 0:1],
                           [("dve", oscb_ready)]))
    c2 = DBuf(P, "c2", 3)
    c2_rel = {}
    cs_rel = {}
    pieces_i = [0]

    def emit_pieces(cf_off, cb_off, n):
        for _ in range(n):
            i = pieces_i[0]
            if i >= len(pieces):
                return
            pieces_i[0] = i + 1
            src_ap, dst_ap, scale_ap, extra = pieces[i]
            s3 = c2.i % 3
            cf = A.view(cf_off + s3 * 4096, 128, 1024, F32)
            lw = c2.load("sp", cf, src_ap,
                         waits=(extra or []) + c2_rel.get(s3, []))
            s2 = cs.i % 2
            cb = A.view(cb_off + s2 * 2048, 128, 1024, BF16)
            if scale_ap is None:
                post = P.op("dve", "tensor_copy", cb, cf,
                            waits=[lw] + cs_rel.get(s2, []), inc=("dve", 1),
                            tag="cvp")
            else:
                post = P.op("dve", "tensor_scalar_mul", cb, cf, scale_ap,
                            waits=[lw] + cs_rel.get(s2, []), inc=("dve", 1),
                            tag="cvps")
            c2_rel[s3] = [("dve", post)]
            sw = cs.load("pool", dst_ap, cb, waits=[("dve", post)])
            cs_rel[s2] = [sw]

    def ln_stats(src, t0, tcols, sm, sr, xfv, xbv, xqv, tev, rel):
        nh = tcols // 512
        assert nh in (1, 2)
        for dc in range(NDC):
            s = dc % 2
            lw = xs.load("pool", xfv[s], src[dc * 128:(dc + 1) * 128, t0:t0 + tcols],
                         waits=rel.get(("f", s), []))
            ap = P.op("act", "activation", out=xqv[s], in_=xfv[s], func=AF.Square,
                      waits=[lw] + rel.get(("q", s), []), inc=("act", 1), tag="lnsq")
            dp = P.op("dve", "tensor_copy", xbv[s], xfv[s],
                      waits=[lw] + rel.get(("b", s), []), inc=("dve", 1), tag="lncp")
            rel[("f", s)] = [("act", ap), ("dve", dp)]
            pb = pq = None
            for h in range(nh):
                c = slice(h * 512, (h + 1) * 512)
                pb = P.op("pe", "matmul", pair(2)[0:1, c], onbf[:, :], xbv[s][:, c],
                          start=(dc == 0), stop=(dc == NDC - 1), waits=[("dve", dp)],
                          inc=("pe", 1), tag="ln_sx")
                pq = P.op("pe", "matmul", pair(3)[0:1, c], onbf[:, :], xqv[s][:, c],
                          start=(dc == 0), stop=(dc == NDC - 1), waits=[("act", ap)],
                          inc=("pe", 1), tag="ln_sq")
            rel[("b", s)] = [("pe", pb)]
            rel[("q", s)] = [("pe", pq)]
        pw = [("pe", P.cnt["pe"])]
        dm = P.op("dve", "tensor_scalar_mul", sm, pair(2)[0:1, 0:tcols], 1.0 / D,
                  waits=pw, inc=("dve", 1), tag="ln_m")
        P.op("dve", "tensor_scalar_mul", tev[0:1, 0:tcols], pair(3)[0:1, 0:tcols],
             1.0 / D, tag="ln_e")
        a2 = P.op("act", "activation", out=tev[0:1, tcols:2 * tcols], in_=sm,
                  func=AF.Square, waits=pw + [("dve", dm)], inc=("act", 1),
                  tag="ln_ms")
        d3 = P.op("dve", "tensor_sub", tev[0:1, 0:tcols], tev[0:1, 0:tcols],
                  tev[0:1, tcols:2 * tcols], waits=[("act", a2)], inc=("dve", 1),
                  tag="ln_v")
        a3 = P.op("act", "activation", out=tev[0:1, tcols:2 * tcols],
                  in_=tev[0:1, 0:tcols], func=AF.Sqrt, bias=cstb[0:1, 0:1],
                  waits=[("dve", d3)], inc=("act", 1), tag="ln_s")
        d4 = P.op("dve", "reciprocal", sr, tev[0:1, tcols:2 * tcols],
                  waits=[("act", a3)], inc=("dve", 1), tag="ln_r")
        return d4

    def ln_bcast(sm, sr, mb, rb, tcols, ready):
        nh = tcols // 512
        last = None
        for h in range(nh):
            c = slice(h * 512, (h + 1) * 512)
            w = [("dve", ready)] + ([("dve", last)] if last else [])
            pm = P.op("pe", "matmul", bank(4), onf[0:1, :], sm[0:1, c],
                      start=True, stop=True, waits=w, inc=("pe", 1), tag="bcm")
            P.op("dve", "tensor_copy", mb[:, c], bank(4), waits=[("pe", pm)],
                 tag="bcmc")
            pr = P.op("pe", "matmul", bank(5), onf[0:1, :], sr[0:1, c],
                      start=True, stop=True, waits=[("dve", ready)], inc=("pe", 1),
                      tag="bcr")
            last = P.op("dve", "tensor_copy", rb[:, c], bank(5), waits=[("pe", pr)],
                        inc=("dve", 1), tag="bcrc")
        return last

    def emit_attn_a(src, dst):
        import collections
        mb = A.view(16 * 1024, 128, T, F32)
        rb = A.view(32 * 1024, 128, T, F32)
        yT = A.view(48 * 1024, 128, NDC * T, BF16)
        ln_dc = A.view(112 * 1024, 128, T, BF16)
        qT = A.view(120 * 1024, 128, T, BF16)
        kT = A.view(128 * 1024, 128, T, BF16)
        v2 = A.view(136 * 1024, 128, 32 * 130, BF16)
        v2r = v2.rearrange("p (a b) -> p a b", b=130)
        xfh = [A.view(145 * 1024 + s * 8 * 1024, 128, 2048, F32) for s in range(2)]
        prb = [A.view(177 * 1024 + s * 2048, 128, 1024, BF16) for s in range(3)]
        tmpf = A.view(183 * 1024, 128, 512, F32)
        dnv2 = A.view(185 * 1024, 1, 1024, F32)
        rbsAB = A.view(189 * 1024, 128, 512, F32)
        tev = A.view(183 * 1024, 1, 1024, F32)
        sxf = [A.view(145 * 1024 + s * 4096, 128, 1024, F32) for s in range(2)]
        sxb = [A.view(153 * 1024 + s * 2048, 128, 1024, BF16) for s in range(2)]
        sxq = [A.view(157 * 1024 + s * 2048, 128, 1024, BF16) for s in range(2)]
        smcr = A.view(189 * 1024, 1, 1024, F32)
        smc = smcr[0:1, 0:512]
        src_c = smcr[0:1, 512:1024]

        lrel = {}
        smcr2 = [smc, src_c]

        def stats_steps(tc):
            t0s = tc * 512
            par = tc % 2
            bx, bq2 = bank(4 + 2 * par), bank(5 + 2 * par)
            stt = {}
            out = []

            def dc_step(i):
                def f():
                    s = i % 2
                    lw = xs.load("pool",
                                 sxf[s].rearrange("p (a t) -> p a t", a=2),
                                 src[2 * i * 128:(2 * i + 2) * 128,
                                     t0s:t0s + 512].rearrange("(a p) t -> p a t",
                                                              p=128),
                                 waits=lrel.get(("f", s), []))
                    ap = P.op("act", "activation", out=sxq[s], in_=sxf[s],
                              func=AF.Square, waits=[lw] + lrel.get(("q", s), []),
                              inc=("act", 1), tag="lnsq")
                    dp = P.op("dve", "tensor_copy", sxb[s], sxf[s],
                              waits=[lw] + lrel.get(("b", s), []),
                              inc=("dve", 1), tag="lncp")
                    lrel[("f", s)] = [("act", ap), ("dve", dp)]
                    pb = pq = None
                    for a in range(2):
                        c = slice(a * 512, (a + 1) * 512)
                        pb = P.op("pe", "matmul", bx[0:1, :], onbf[:, :],
                                  sxb[s][:, c], start=(i == 0 and a == 0),
                                  stop=(i == 3 and a == 1),
                                  waits=[("dve", dp)], inc=("pe", 1), tag="ln_sx")
                        pq = P.op("pe", "matmul", bq2[0:1, :], onbf[:, :],
                                  sxq[s][:, c], start=(i == 0 and a == 0),
                                  stop=(i == 3 and a == 1),
                                  waits=[("act", ap)], inc=("pe", 1), tag="ln_sq")
                    lrel[("b", s)] = [("pe", pb)]
                    lrel[("q", s)] = [("pe", pq)]
                    if i == 3:
                        stt["pelast"] = max(pb, pq)
                return f

            def math():
                pw = [("pe", stt["pelast"])] + lrel.get(("m", par), []) + \
                    lrel.get("bcg", [])
                dm = P.op("dve", "tensor_scalar_mul", smcr2[0][0:1, :], bx[0:1, :],
                          1.0 / D, waits=pw, inc=("dve", 1), tag="ln_m")
                P.op("dve", "tensor_scalar_mul", tev[0:1, 0:512], bq2[0:1, :],
                     1.0 / D, tag="ln_e")
                a2 = P.op("act", "activation", out=tev[0:1, 512:1024],
                          in_=smcr2[0][0:1, :], func=AF.Square,
                          waits=pw + [("dve", dm)], inc=("act", 1), tag="ln_ms")
                d3 = P.op("dve", "tensor_sub", tev[0:1, 0:512], tev[0:1, 0:512],
                          tev[0:1, 512:1024], waits=[("act", a2)], inc=("dve", 1),
                          tag="ln_v")
                a3 = P.op("act", "activation", out=tev[0:1, 512:1024],
                          in_=tev[0:1, 0:512], func=AF.Sqrt, bias=cstb[0:1, 0:1],
                          waits=[("dve", d3)], inc=("act", 1), tag="ln_s")
                d4 = P.op("dve", "reciprocal", smcr2[1][0:1, :], tev[0:1, 512:1024],
                          waits=[("act", a3)], inc=("dve", 1), tag="ln_r")
                stt["rdy"] = d4

            def bc():
                c = slice(t0s, t0s + 512)
                pm = P.op("pe", "matmul", bx, onf[0:1, :], smcr2[0][0:1, :],
                          start=True, stop=True, waits=[("dve", stt["rdy"])],
                          inc=("pe", 1), tag="bcm")
                P.op("dve", "tensor_copy", mb[:, c], bx, waits=[("pe", pm)],
                     tag="bcmc")
                pr = P.op("pe", "matmul", bq2, onf[0:1, :], smcr2[1][0:1, :],
                          start=True, stop=True, waits=[("dve", stt["rdy"])],
                          inc=("pe", 1), tag="bcr")
                last = P.op("dve", "tensor_copy", rb[:, c], bq2, waits=[("pe", pr)],
                            inc=("dve", 1), tag="bcrc")
                lrel[("m", par)] = [("dve", last)]
                lrel["bcg"] = [("pe", pr)]

            return [dc_step(d) for d in range(4)], math, bc

        prev = None
        for tc in range(8):
            steps, mth, bcf = stats_steps(tc)
            for j, f in enumerate(steps):
                f()
                if prev is not None:
                    if j == 0:
                        prev[0]()
                    elif j == 1:
                        prev[1]()
            prev = (mth, bcf)
        prev[0]()
        prev[1]()
        P.barrier()

        P.op("dve", "memset", v2r[:, :, 64:65], 1.0, tag="v2o1")
        v2o = P.op("dve", "memset", v2r[:, :, 129:130], 1.0, inc=("dve", 1), tag="v2o2")
        P.after("pe", [("dve", v2o)])

        xrel = {}
        ln_guard = [0]

        def emit_norm_chunk(dc, tc):
            h = tc // 4
            if tc % 4 == 0:
                w = xrel.get(h, []) + ([("pe", ln_guard[0])] if tc == 0 else [])
                xrel[("lw", h)] = xl.load(
                    "pool", xfh[h], src[dc * 128:(dc + 1) * 128,
                                        h * 2048:(h + 1) * 2048], waits=w)
            c = slice(tc * 512, tc * 512 + 512)
            cc = slice((tc % 4) * 512, (tc % 4) * 512 + 512)
            w = [xrel[("lw", h)]]
            if tc == 0:
                w.append(("pe", ln_guard[0]))
            P.op("dve", "tensor_sub", tmpf[:, :], xfh[h][:, cc], mb[:, c],
                 waits=w, tag="a_sb")
            post = P.op("dve", "tensor_mul", ln_dc[:, c], tmpf[:, :], rb[:, c],
                        inc=("dve", 1), tag="a_ml")
            xrel[h] = [("dve", post)]
            return post

        ln_done = None
        for tc in range(8):
            ln_done = emit_norm_chunk(0, tc)
        ln_ready = {0: ln_done}

        pair_rel = {}
        slot_rel = {}
        ybank_rel = []
        abc_rel = {}
        pbc_last = []
        tailq = collections.deque()

        def drain_tail(n):
            for _ in range(min(n, len(tailq))):
                tailq.popleft()()

        for dc in range(NDC):
            last_ln = ln_ready[dc]
            qk_rel = {}
            qk_done = last_ln
            for tc in range(8):
                c = slice(tc * 512, tc * 512 + 512)
                bq, bk = bank(tc % 2), bank(2 + tc % 2)
                pq = P.op("pe", "matmul", bq, wqab[:, :], ln_dc[:, c],
                          start=True, stop=True,
                          waits=[("dve", last_ln)] + qk_rel.get(("q", tc % 2), []),
                          inc=("pe", 1), tag="a_q")
                dq = P.op("act", "activation", out=qT[:, c], in_=bq, func=AF.Copy,
                          waits=[("pe", pq)], inc=("act", 1), tag="a_qc")
                qk_rel[("q", tc % 2)] = [("act", dq)]
                pk = P.op("pe", "matmul", bk, wkab[:, :], ln_dc[:, c],
                          start=True, stop=True, waits=qk_rel.get(("k", tc % 2), []),
                          inc=("pe", 1), tag="a_k")
                dk = P.op("dve", "tensor_copy", kT[:, c], bk, waits=[("pe", pk)],
                          inc=("dve", 1), tag="a_kc")
                qk_rel[("k", tc % 2)] = [("dve", dk)]
                qk_done = dk
                drain_tail(1)
            drain_tail(8)
            v_rel = {}
            v_done = qk_done
            pv = None
            for tb in range(32):
                bv = bank(4 + tb % 2)
                wv = [("dve", last_ln)] + v_rel.get(tb % 2, [])
                if tb < 2 and ybank_rel:
                    wv += ybank_rel
                pv = P.op("pe", "matmul", bv[:, 0:128],
                          ln_dc[:, tb * 128:tb * 128 + 128], wvab[:, :],
                          start=True, stop=True, waits=wv, inc=("pe", 1), tag="a_v")
                dst_v = v2r[:, tb, :].rearrange("p (cc dd) -> p cc dd",
                                                cc=2, dd=65)[:, :, 0:64]
                src_v = bv[:, 0:128].rearrange("p (cc dd) -> p cc dd", cc=2, dd=64)
                dv = P.op("dve", "tensor_copy", dst_v, src_v, waits=[("pe", pv)],
                          inc=("dve", 1), tag="a_vc")
                v_rel[tb % 2] = [("dve", dv)]
                v_done = dv
            ln_guard[0] = pv

            for tc in range(8):
                nkb = 4 * tc + 4
                qc = slice(tc * 512, tc * 512 + 512)
                probs = {}
                ypost = {}
                sB = {}

                def emit_scores(kb):
                    pp = pair(kb % 2)
                    kc = slice(kb * 128, kb * 128 + 128)
                    wv = [("dve", max(qk_done, v_done))] + pair_rel.get(kb % 2, [])
                    P.op("pe", "matmul", pp[:, 0:512], kT[0:64, kc], qT[0:64, qc],
                         start=True, stop=True, waits=wv, tag="a_sA")
                    sB[kb] = P.op("pe", "matmul", pp[:, 512:1024], kT[64:128, kc],
                                  qT[64:128, qc], start=True, stop=True,
                                  inc=("pe", 1), tag="a_sB")

                def emit_probs(kb):
                    pp = pair(kb % 2)
                    slot = prb[kb % 3]
                    wv = [("pe", sB[kb])] + slot_rel.get(kb % 3, [])
                    apost = P.op("act", "activation", out=slot[:, :], in_=pp[:, :],
                                 func=AF.Exp, bias=cstb[:, 1:2], waits=wv,
                                 inc=("act", 1), tag="a_ex")
                    pair_rel[kb % 2] = [("act", apost)]
                    if kb < 4 * tc:
                        probs[kb] = (apost, None)
                    else:
                        c0 = 384 - (kb * 128 - tc * 512)
                        P.op("dve", "tensor_mul", slot[:, 0:512], slot[:, 0:512],
                             t0m[:, c0:c0 + 512], waits=[("act", apost)], tag="a_mA")
                        dpost = P.op("dve", "tensor_mul", slot[:, 512:1024],
                                     slot[:, 512:1024], t0m[:, c0:c0 + 512],
                                     inc=("dve", 1), tag="a_mB")
                        probs[kb] = (apost, dpost)

                def emit_y(kb):
                    slot = prb[kb % 3]
                    wv = [("act", probs[kb][0])]
                    if probs[kb][1] is not None:
                        wv.append(("dve", probs[kb][1]))
                    if kb == 0 and ybank_rel:
                        wv += ybank_rel
                    P.op("pe", "matmul", bank(4)[0:65, :],
                         v2[:, kb * 130:kb * 130 + 65], slot[:, 0:512],
                         start=(kb == 0), stop=(kb == nkb - 1), waits=wv, tag="a_yA")
                    ypost[kb] = P.op("pe", "matmul", bank(5)[0:65, :],
                                     v2[:, kb * 130 + 65:kb * 130 + 130],
                                     slot[:, 512:1024],
                                     start=(kb == 0), stop=(kb == nkb - 1),
                                     inc=("pe", 1), tag="a_yB")
                    slot_rel[kb % 3] = [("pe", ypost[kb])]

                for kb in range(nkb):
                    emit_scores(kb)
                    emit_probs(kb)
                    drain_tail(2)
                    if kb == 2:
                        if dc + 1 < NDC and tc < 4:
                            emit_norm_chunk(dc + 1, 2 * tc)
                            p = emit_norm_chunk(dc + 1, 2 * tc + 1)
                            if tc == 3:
                                ln_ready[dc + 1] = p
                        emit_pieces(161 * 1024, 173 * 1024, 3)
                    if kb >= 2:
                        emit_y(kb - 2)
                emit_y(nkb - 2)
                emit_y(nkb - 1)

                def make_tail(dc=dc, tc=tc, qc=qc, ystop=ypost[nkb - 1]):
                    def dn(hd):
                        def f():
                            yb = bank(4 + hd)
                            post = P.op("dve", "reciprocal",
                                        dnv2[0:1, hd * 512:hd * 512 + 512],
                                        yb[64:65, :],
                                        waits=[("pe", ystop)] + (pbc_last if hd == 0 else []),
                                        inc=("dve", 1), tag="a_dn")
                            tail_state[("dn", hd)] = post
                        return f

                    def pbc(hd):
                        def f():
                            bc = bank(6 + hd)
                            post = P.op("pe", "matmul", bc[0:64, :], onf[0:1, 0:64],
                                        dnv2[0:1, hd * 512:hd * 512 + 512],
                                        start=True, stop=True,
                                        waits=[("dve", tail_state[("dn", hd)])],
                                        inc=("pe", 1), tag="a_db")
                            tail_state[("pbc", hd)] = post
                            if hd == 1:
                                pbc_last[:] = [("pe", post)]
                        return f

                    def abc(hd):
                        def f():
                            bc = bank(6 + hd)
                            rbs = rbsAB[hd * 64:hd * 64 + 64, :]
                            post = P.op("act", "activation", out=rbs, in_=bc[0:64, :],
                                        func=AF.Copy,
                                        waits=[("pe", tail_state[("pbc", hd)])] +
                                        abc_rel.get(hd, []),
                                        inc=("act", 1), tag="a_dc")
                            tail_state[("abc", hd)] = post
                        return f

                    def dty(hd):
                        def f():
                            yb = bank(4 + hd)
                            rbs = rbsAB[hd * 64:hd * 64 + 64, :]
                            ydst = yT[:, dc * T:dc * T + T][hd * 64:hd * 64 + 64, qc]
                            post = P.op("dve", "tensor_mul", ydst, yb[0:64, :], rbs,
                                        waits=[("act", tail_state[("abc", hd)])],
                                        inc=("dve", 1), tag="a_ye")
                            abc_rel[hd] = [("dve", post)]
                            tail_state[("dty", hd)] = post
                            if hd == 1:
                                ybank_rel[:] = [("dve", tail_state[("dty", 0)]),
                                                ("dve", post)]
                        return f

                    return [dn(0), dn(1), pbc(0), pbc(1), abc(0),
                            abc(1), dty(0), dty(1)]

                tail_state = {}
                tailq.extend(make_tail())
                drain_tail(2)
        drain_tail(99)

        P.barrier()
        otf = [A.view(145 * 1024 + s * 2048, 128, 512, F32) for s in range(2)]
        xrf = [A.view(149 * 1024 + s * 2048, 128, 512, F32) for s in range(2)]
        ev = {}
        ot_rel = {}
        xr_rel = {}
        i = 0
        for dcO in range(NDC):
            for tc in range(8):
                qc = slice(tc * 512, tc * 512 + 512)
                s = i % 2
                bnk = bank(i % 4)
                p1 = None
                for ic in range(NDC):
                    p1 = P.op("pe", "matmul", bnk,
                              wu_sb[:, ic * 1024 + dcO * 128: ic * 1024 + dcO * 128 + 128],
                              yT[:, ic * T:ic * T + T][:, qc],
                              start=(ic == 0), stop=(ic == NDC - 1),
                              waits=ev.get(i % 4, []) if ic == 0 else (),
                              inc=("pe", 1) if ic == NDC - 1 else None, tag="a_wu")
                lw = xq.load("pool", xrf[s], src[dcO * 128:(dcO + 1) * 128, qc],
                             waits=xr_rel.get(s, []))
                dp = P.op("dve", "scalar_tensor_tensor", otf[s][:, :], bnk,
                          busb[:, dcO:dcO + 1], xrf[s][:, :], ALU.add, ALU.add,
                          waits=[("pe", p1), lw] + ot_rel.get(s, []),
                          inc=("dve", 1), tag="a_e")
                ev[i % 4] = [("dve", dp)]
                sw = st.load("pool", dst[dcO * 128:(dcO + 1) * 128, qc], otf[s][:, :],
                             waits=[("dve", dp)])
                ot_rel[s] = [sw]
                xr_rel[s] = [("dve", dp)]
                emit_pieces(161 * 1024, 173 * 1024, 2)
                i += 1
        emit_pieces(161 * 1024, 173 * 1024, 10 ** 9)
        P.barrier()

    def emit_ffn(src, dst):
        import collections
        a_sb = A.view(0, 128, NHC * 1024, BF16)
        g1 = A.view(64 * 1024, 128, NHC * 1024, BF16)
        lnv = A.view(128 * 1024, 128, NDC * 1024, BF16)
        smx = A.view(128 * 1024, 1, 1024, F32)
        srx = A.view(132 * 1024, 1, 1024, F32)
        tev = A.view(136 * 1024, 1, 2048, F32)
        mbb = A.view(144 * 1024, 128, 1024, F32)
        rbb = A.view(148 * 1024, 128, 1024, F32)
        w1sl = [A.view(152 * 1024 + s * 2048, 128, 1024, BF16) for s in range(2)]
        w2sl = [A.view(156 * 1024 + s * 2048, 128, 1024, BF16) for s in range(2)]
        hsl = [A.view(160 * 1024 + s * 8192, 128, 4096, BF16) for s in range(2)]
        siv = [A.view(176 * 1024 + s * 1024, 128, 512, BF16) for s in range(2)]
        xfv = [A.view(178 * 1024 + s * 4096, 128, 1024, F32) for s in range(2)]
        sxb = [A.view(186 * 1024 + s * 2048, 128, 1024, BF16) for s in range(2)]
        sxqv = A.view(190 * 1024, 128, 1024, BF16)
        otf = [A.view(192 * 1024 + s * 2048, 128, 512, F32) for s in range(2)]

        bg = collections.deque()

        def drain(n):
            for _ in range(min(n, len(bg))):
                bg.popleft()()

        lrel = {}
        xshare = {}
        ln_ready = {}
        bank_rel = {}

        def queue_stats(tc):
            t0 = tc * 1024
            st = {}

            def dc_step(dc):
                def f():
                    s = dc % 2
                    lw = xs.load("pool", xfv[s],
                                 src[dc * 128:(dc + 1) * 128, t0:t0 + 1024],
                                 waits=xshare.get(s, []))
                    ap = P.op("act", "activation", out=sxqv, in_=xfv[s],
                              func=AF.Square, waits=[lw] + lrel.get("qq", []),
                              inc=("act", 1), tag="lnsq")
                    dp = P.op("dve", "tensor_copy", sxb[s], xfv[s],
                              waits=[lw] + lrel.get(("b", s), []),
                              inc=("dve", 1), tag="lncp")
                    xshare[s] = [("act", ap), ("dve", dp)]
                    pb = pq = None
                    for h in range(2):
                        c = slice(h * 512, (h + 1) * 512)
                        pb = P.op("pe", "matmul", pair(2)[0:1, c], onbf[:, :],
                                  sxb[s][:, c], start=(dc == 0), stop=(dc == NDC - 1),
                                  waits=[("dve", dp)], inc=("pe", 1), tag="ln_sx")
                        pq = P.op("pe", "matmul", pair(3)[0:1, c], onbf[:, :],
                                  sxqv[:, c], start=(dc == 0), stop=(dc == NDC - 1),
                                  waits=[("act", ap)], inc=("pe", 1), tag="ln_sq")
                    lrel[("b", s)] = [("pe", pb)]
                    lrel["qq"] = [("pe", pq)]
                    if dc == NDC - 1:
                        st["pelast"] = max(pb, pq)
                return f

            def math():
                pw = [("pe", st["pelast"])]
                dm = P.op("dve", "tensor_scalar_mul", smx[0:1, :], pair(2)[0:1, 0:1024],
                          1.0 / D, waits=pw, inc=("dve", 1), tag="ln_m")
                P.op("dve", "tensor_scalar_mul", tev[0:1, 0:1024],
                     pair(3)[0:1, 0:1024], 1.0 / D, tag="ln_e")
                a2 = P.op("act", "activation", out=tev[0:1, 1024:2048],
                          in_=smx[0:1, :], func=AF.Square,
                          waits=pw + [("dve", dm)], inc=("act", 1), tag="ln_ms")
                d3 = P.op("dve", "tensor_sub", tev[0:1, 0:1024], tev[0:1, 0:1024],
                          tev[0:1, 1024:2048], waits=[("act", a2)], inc=("dve", 1),
                          tag="ln_v")
                a3 = P.op("act", "activation", out=tev[0:1, 1024:2048],
                          in_=tev[0:1, 0:1024], func=AF.Sqrt, bias=cstb[0:1, 0:1],
                          waits=[("dve", d3)], inc=("act", 1), tag="ln_s")
                d4 = P.op("dve", "reciprocal", srx[0:1, :], tev[0:1, 1024:2048],
                          waits=[("act", a3)], inc=("dve", 1), tag="ln_r")
                st["rdy"] = d4

            def bc(h):
                def f():
                    c = slice(h * 512, (h + 1) * 512)
                    pm = P.op("pe", "matmul", bank(4), onf[0:1, :], smx[0:1, c],
                              start=True, stop=True,
                              waits=[("dve", st["rdy"])] + st.get("bl", []),
                              inc=("pe", 1), tag="bcm")
                    P.op("dve", "tensor_copy", mbb[:, c], bank(4),
                         waits=[("pe", pm)], tag="bcmc")
                    pr = P.op("pe", "matmul", bank(5), onf[0:1, :], srx[0:1, c],
                              start=True, stop=True, waits=[("dve", st["rdy"])],
                              inc=("pe", 1), tag="bcr")
                    last = P.op("dve", "tensor_copy", rbb[:, c], bank(5),
                                waits=[("pe", pr)], inc=("dve", 1), tag="bcrc")
                    st["bl"] = [("dve", last)]
                    st["done"] = last
                return f

            return [dc_step(d) for d in range(NDC)] + [math, bc(0), bc(1)], st

        def queue_norm(tc, wfree, stt):
            t0 = tc * 1024

            def step(dc):
                def f():
                    s = dc % 2
                    lw = xl.load("pool", xfv[s],
                                 src[dc * 128:(dc + 1) * 128, t0:t0 + 1024],
                                 waits=[("dve", stt["done"])] + xshare.get(s, []))
                    P.op("dve", "tensor_sub", xfv[s][:, :], xfv[s][:, :], mbb[:, :],
                         waits=[lw] + ([("pe", wfree)] if dc == 0 else []), tag="f_sb")
                    post = P.op("dve", "tensor_mul", lnv[:, dc * 1024:(dc + 1) * 1024],
                                xfv[s][:, :], rbb[:, :], inc=("dve", 1), tag="f_ml")
                    xshare[s] = [("dve", post)]
                    ln_ready[tc] = post
                return f

            return [step(d) for d in range(NDC)]

        steps, stt = queue_stats(0)
        for f in steps:
            f()
        for f in queue_norm(0, 0, stt):
            f()

        for tc in range(4):
            t0 = tc * 1024
            last_ln = ln_ready[tc]
            wrel = {}
            p1 = p2 = None
            for oc in range(NHC):
                s = oc % 2
                lw1 = wl.load("sp", w1sl[s][:, :], w1s[oc, :, :],
                              waits=wrel.get(("1", s), []))
                lw2 = w2l.load("sp", w2sl[s][:, :], w2s[oc, :, :],
                               waits=wrel.get(("2", s), []))
                for hf in range(2):
                    j = oc * 2 + hf
                    b1i, b2i = j % 2, 2 + j % 2
                    b1, b2 = bank(b1i), bank(b2i)
                    for ic in range(NDC):
                        p1 = P.op("pe", "matmul", b1, w1sl[s][:, ic * 128:(ic + 1) * 128],
                                  lnv[:, ic * 1024 + hf * 512: ic * 1024 + hf * 512 + 512],
                                  start=(ic == 0), stop=(ic == NDC - 1),
                                  waits=[lw1, ("dve", last_ln)] + bank_rel.get(b1i, [])
                                  if ic == 0 else (),
                                  inc=("pe", 1) if ic == NDC - 1 else None, tag="f_w1")
                    sip = P.op("act", "activation", out=siv[hf][:, :], in_=b1,
                               func=AF.Silu, scale=s1b[:, oc:oc + 1],
                               waits=[("pe", p1)] + bank_rel.get(("si", hf), []),
                               inc=("act", 1), tag="f_si")
                    bank_rel[b1i] = [("act", sip)]
                    for ic in range(NDC):
                        p2 = P.op("pe", "matmul", b2, w2sl[s][:, ic * 128:(ic + 1) * 128],
                                  lnv[:, ic * 1024 + hf * 512: ic * 1024 + hf * 512 + 512],
                                  start=(ic == 0), stop=(ic == NDC - 1),
                                  waits=[lw2] + bank_rel.get(b2i, []) if ic == 0 else (),
                                  inc=("pe", 1) if ic == NDC - 1 else None, tag="f_w2")
                    gp = P.op("dve", "scalar_tensor_tensor",
                              a_sb[:, oc * 1024 + hf * 512: oc * 1024 + hf * 512 + 512],
                              b2, s2b[:, oc:oc + 1], siv[hf][:, :], ALU.mult, ALU.mult,
                              waits=[("pe", p2), ("act", sip)], inc=("dve", 1),
                              tag="f_ga")
                    bank_rel[b2i] = [("dve", gp)]
                    bank_rel[("si", hf)] = [("dve", gp)]
                wrel[("1", s)] = [("pe", p1)]
                wrel[("2", s)] = [("pe", p2)]
            a_done = ("dve", P.cnt["dve"])
            w2_free = p2

            if tc + 1 < 4:
                steps, stt = queue_stats(tc + 1)
                bg.extend(steps)
                norm_steps = queue_norm(tc + 1, w2_free, stt)
            else:
                norm_steps = []

            def h_layer(wsrc, rsrc, wdst, sv, ready):
                wrel2 = {}
                gp = None
                p1 = None
                for oc in range(NHC):
                    s = oc % 2
                    lw1 = wl.load("sp", hsl[s][:, :], wsrc[oc, :, :],
                                  waits=wrel2.get(s, [ready] if oc < 2 else []))
                    for hf in range(2):
                        j = oc * 2 + hf
                        bi = j % 4
                        b1 = bank(bi)
                        for ic in range(NHC):
                            p1 = P.op("pe", "matmul", b1,
                                      hsl[s][:, ic * 128:(ic + 1) * 128],
                                      rsrc[:, ic * 1024 + hf * 512: ic * 1024 + hf * 512 + 512],
                                      start=(ic == 0), stop=(ic == NHC - 1),
                                      waits=[lw1, ready] + bank_rel.get(bi, [])
                                      if ic == 0 else (),
                                      inc=("pe", 1) if ic == NHC - 1 else None,
                                      tag="f_h")
                        gp = P.op("act", "activation",
                                  out=wdst[:, oc * 1024 + hf * 512: oc * 1024 + hf * 512 + 512],
                                  in_=b1, func=AF.Gelu, scale=sv[:, oc:oc + 1],
                                  waits=[("pe", p1)], inc=("act", 1), tag="f_ge")
                        bank_rel[bi] = [("act", gp)]
                    wrel2[s] = [("pe", p1)]
                    drain(1)
                return ("act", gp)

            g1_done = h_layer(h1s, a_sb, g1, sh1b, a_done)
            bg.extend(norm_steps)
            g2_done = h_layer(h2s, g1, a_sb, sh2b, g1_done)
            drain(99)

            ot_rel = {}
            wrel3 = {}
            for oc in range(NDC):
                s = oc % 2
                lw1 = wl.load("sp", hsl[s][:, :], w3s[oc, :, :],
                              waits=wrel3.get(s, [g2_done] if oc < 2 else []))
                lw = xq.load("pool", xfv[s][:, :],
                             src[oc * 128:(oc + 1) * 128, t0:t0 + 1024],
                             waits=xshare.get(s, [a_done]))
                dp = None
                p1 = None
                for hf in range(2):
                    j = oc * 2 + hf
                    bi = j % 4
                    b1 = bank(bi)
                    for ic in range(NHC):
                        p1 = P.op("pe", "matmul", b1, hsl[s][:, ic * 128:(ic + 1) * 128],
                                  a_sb[:, ic * 1024 + hf * 512: ic * 1024 + hf * 512 + 512],
                                  start=(ic == 0), stop=(ic == NHC - 1),
                                  waits=[lw1, g2_done] + bank_rel.get(bi, [])
                                  if ic == 0 else (),
                                  inc=("pe", 1) if ic == NHC - 1 else None, tag="f_w3")
                    dp = P.op("dve", "scalar_tensor_tensor", otf[hf][:, :], b1,
                              s3b[:, oc:oc + 1], xfv[s][:, hf * 512:hf * 512 + 512],
                              ALU.mult, ALU.add,
                              waits=[("pe", p1), lw] + ot_rel.get(hf, []),
                              inc=("dve", 1), tag="f_e")
                    bank_rel[bi] = [("dve", dp)]
                    sw = st.load("pool",
                                 dst[oc * 128:(oc + 1) * 128,
                                     t0 + hf * 512:t0 + hf * 512 + 512],
                                 otf[hf][:, :], waits=[("dve", dp)])
                    ot_rel[hf] = [sw]
                xshare[s] = [("dve", dp)]
                wrel3[s] = [("pe", p1)]
        P.barrier()

    def emit_attn_b(src, dst, qkv_only=False):
        import collections
        lnb2 = [A.view(s * 8 * 1024, 128, NDC * 512, BF16) for s in range(2)]
        wqr = A.view(16 * 1024, 128, NDC * 1024, BF16)
        wkr = A.view(32 * 1024, 128, NDC * 1024, BF16)
        wvr = A.view(48 * 1024, 128, NDC * 1024, BF16)
        qsb = A.view(64 * 1024, 128, NDC * 512, BF16)
        ksb = A.view(72 * 1024, 128, NDC * 512, BF16)
        mbb = A.view(80 * 1024, 128, 512, F32)
        rbb = A.view(82 * 1024, 128, 512, F32)
        xfv = [A.view(84 * 1024 + s * 2048, 128, 512, F32) for s in range(2)]
        smx = A.view(88 * 1024, 1, 512, F32)
        srx = A.view(90 * 1024, 1, 512, F32)
        sxb = [A.view(92 * 1024 + s * 1024, 128, 512, BF16) for s in range(2)]
        sxqv = A.view(94 * 1024, 128, 512, BF16)
        tev = A.view(96 * 1024, 1, 1024, F32)
        vrow = [A.view(100 * 1024 + s * 2048, 128, 1024, BF16) for s in range(2)]

        P.sem("bw")
        for ic in range(NDC):
            sl = slice(ic * 128, (ic + 1) * 128)
            P.op("sp", "dma_start", out=wqr[:, ic * 1024:(ic + 1) * 1024],
                 in_=wqbs[sl, :], inc=("bw", 16), tag="bwq")
            P.op("sp", "dma_start", out=wkr[:, ic * 1024:(ic + 1) * 1024],
                 in_=wkbs[sl, :], inc=("bw", 16), tag="bwk")
            P.op("sp", "dma_start", out=wvr[:, ic * 1024:(ic + 1) * 1024],
                 in_=wvbs[sl, :], inc=("bw", 16), tag="bwv")
        BW = [("bw", P.cnt["bw"])]

        bg = collections.deque()

        def drain(n):
            for _ in range(min(n, len(bg))):
                bg.popleft()()

        lrel = {}
        xshare = {}
        ln_ready = {}
        pv_guard = {}

        def queue_stats4(tc):
            t0 = tc * 512
            st = {}

            def dc_step(dc):
                def f():
                    s = dc % 2
                    lw = xs.load("pool", xfv[s],
                                 src[dc * 128:(dc + 1) * 128, t0:t0 + 512],
                                 waits=xshare.get(s, []))
                    ap = P.op("act", "activation", out=sxqv, in_=xfv[s],
                              func=AF.Square, waits=[lw] + lrel.get("qq", []),
                              inc=("act", 1), tag="lnsq")
                    dp = P.op("dve", "tensor_copy", sxb[s], xfv[s],
                              waits=[lw] + lrel.get(("b", s), []),
                              inc=("dve", 1), tag="lncp")
                    xshare[s] = [("act", ap), ("dve", dp)]
                    pb = P.op("pe", "matmul", bank(4)[0:1, :], onbf[:, :], sxb[s],
                              start=(dc == 0), stop=(dc == NDC - 1),
                              waits=[("dve", dp)], inc=("pe", 1), tag="ln_sx")
                    pq = P.op("pe", "matmul", bank(5)[0:1, :], onbf[:, :], sxqv,
                              start=(dc == 0), stop=(dc == NDC - 1),
                              waits=[("act", ap)], inc=("pe", 1), tag="ln_sq")
                    lrel[("b", s)] = [("pe", pb)]
                    lrel["qq"] = [("pe", pq)]
                    if dc == NDC - 1:
                        st["pelast"] = max(pb, pq)
                return f

            def math():
                pw = [("pe", st["pelast"])]
                dm = P.op("dve", "tensor_scalar_mul", smx[0:1, :], bank(4)[0:1, :],
                          1.0 / D, waits=pw, inc=("dve", 1), tag="ln_m")
                P.op("dve", "tensor_scalar_mul", tev[0:1, 0:512], bank(5)[0:1, :],
                     1.0 / D, tag="ln_e")
                a2 = P.op("act", "activation", out=tev[0:1, 512:1024],
                          in_=smx[0:1, :], func=AF.Square,
                          waits=pw + [("dve", dm)], inc=("act", 1), tag="ln_ms")
                d3 = P.op("dve", "tensor_sub", tev[0:1, 0:512], tev[0:1, 0:512],
                          tev[0:1, 512:1024], waits=[("act", a2)], inc=("dve", 1),
                          tag="ln_v")
                a3 = P.op("act", "activation", out=tev[0:1, 512:1024],
                          in_=tev[0:1, 0:512], func=AF.Sqrt, bias=cstb[0:1, 0:1],
                          waits=[("dve", d3)], inc=("act", 1), tag="ln_s")
                d4 = P.op("dve", "reciprocal", srx[0:1, :], tev[0:1, 512:1024],
                          waits=[("act", a3)], inc=("dve", 1), tag="ln_r")
                st["rdy"] = d4

            def bc():
                pm = P.op("pe", "matmul", bank(4), onf[0:1, :], smx[0:1, :],
                          start=True, stop=True,
                          waits=[("dve", st["rdy"])] + st.get("bl", []),
                          inc=("pe", 1), tag="bcm")
                P.op("dve", "tensor_copy", mbb[:, :], bank(4), waits=[("pe", pm)],
                     tag="bcmc")
                pr = P.op("pe", "matmul", bank(5), onf[0:1, :], srx[0:1, :],
                          start=True, stop=True, waits=[("dve", st["rdy"])],
                          inc=("pe", 1), tag="bcr")
                last = P.op("dve", "tensor_copy", rbb[:, :], bank(5),
                            waits=[("pe", pr)], inc=("dve", 1), tag="bcrc")
                st["done"] = last

            return [dc_step(d) for d in range(NDC)] + [math, bc], st

        def queue_norm4(tc, stt):
            t0 = tc * 512
            lnb = lnb2[tc % 2]

            def step(dc):
                def f():
                    s = dc % 2
                    w = [("dve", stt["done"])] + xshare.get(s, [])
                    if dc == 0 and (tc - 2) in pv_guard:
                        w.append(("pe", pv_guard[tc - 2]))
                    lw = xl.load("pool", xfv[s],
                                 src[dc * 128:(dc + 1) * 128, t0:t0 + 512], waits=w)
                    P.op("dve", "tensor_sub", xfv[s][:, :], xfv[s][:, :], mbb[:, :],
                         waits=[lw] + ([("pe", pv_guard[tc - 2])]
                                       if (dc == 0 and (tc - 2) in pv_guard) else []),
                         tag="b_sb")
                    post = P.op("dve", "tensor_mul", lnb[:, dc * 512:(dc + 1) * 512],
                                xfv[s][:, :], rbb[:, :], inc=("dve", 1), tag="b_ml")
                    xshare[s] = [("dve", post)]
                    ln_ready[tc] = post
                return f

            return [step(d) for d in range(NDC)]

        steps, stt = queue_stats4(0)
        for f in steps:
            f()
        for f in queue_norm4(0, stt):
            f()

        qrel = {}
        for tc in range(8):
            t0 = tc * 512
            lnb = lnb2[tc % 2]
            if tc not in ln_ready:
                drain(99)
            last_ln = ln_ready[tc]
            if tc + 1 < 8:
                steps, stt = queue_stats4(tc + 1)
                bg.extend(steps)
                bg.extend(queue_norm4(tc + 1, stt))
            qk_rel = {}
            dq = dk = None
            for oc in range(NDC):
                bq, bk = bank(oc % 2), bank(2 + oc % 2)
                pq = pk = None
                for ic in range(NDC):
                    pq = P.op("pe", "matmul", bq,
                              wqr[:, ic * 1024 + oc * 128: ic * 1024 + oc * 128 + 128],
                              lnb[:, ic * 512:(ic + 1) * 512],
                              start=(ic == 0), stop=(ic == NDC - 1),
                              waits=BW + [("dve", last_ln)] + qk_rel.get(("q", oc % 2), [])
                              if ic == 0 else (),
                              inc=("pe", 1) if ic == NDC - 1 else None, tag="b_q")
                dq = P.op("act", "activation", out=qsb[:, oc * 512:(oc + 1) * 512],
                          in_=bq, func=AF.Copy,
                          waits=[("pe", pq)] + (qrel.get("q", []) if oc == 0 else []),
                          inc=("act", 1), tag="b_qc")
                qk_rel[("q", oc % 2)] = [("act", dq)]
                for ic in range(NDC):
                    pk = P.op("pe", "matmul", bk,
                              wkr[:, ic * 1024 + oc * 128: ic * 1024 + oc * 128 + 128],
                              lnb[:, ic * 512:(ic + 1) * 512],
                              start=(ic == 0), stop=(ic == NDC - 1),
                              waits=qk_rel.get(("k", oc % 2), []) if ic == 0 else (),
                              inc=("pe", 1) if ic == NDC - 1 else None, tag="b_k")
                dk = P.op("dve", "tensor_copy", ksb[:, oc * 512:(oc + 1) * 512], bk,
                          waits=[("pe", pk)] + (qrel.get("k", []) if oc == 0 else []),
                          inc=("dve", 1), tag="b_kc")
                qk_rel[("k", oc % 2)] = [("dve", dk)]
                drain(2)
            qk_ev = ("dve", P.cnt["dve"])
            qk_ev_a = ("act", P.cnt["act"])
            sq = qst.load("pool", qS[:, :, t0:t0 + 512].rearrange("a p t -> p a t"),
                          qsb.rearrange("p (a t) -> p a t", t=512),
                          waits=[qk_ev, qk_ev_a])
            sk = qst.load("pool", kS[:, :, t0:t0 + 512].rearrange("a p t -> p a t"),
                          ksb.rearrange("p (a t) -> p a t", t=512), waits=[qk_ev])
            qrel["q"], qrel["k"] = [sq], [sk]
            pv = None
            for tb in range(4):
                s = tb % 2
                vdone = None
                for hf in range(2):
                    bv = bank(6 + (tb * 2 + hf) % 2)
                    wv = BW + [("dve", last_ln)] + \
                        qrel.get(("vb", (tb * 2 + hf) % 2), [])
                    for ic in range(NDC):
                        pv = P.op("pe", "matmul", bv,
                                  lnb[:, ic * 512 + tb * 128: ic * 512 + tb * 128 + 128],
                                  wvr[:, ic * 1024 + hf * 512: ic * 1024 + hf * 512 + 512],
                                  start=(ic == 0), stop=(ic == NDC - 1),
                                  waits=wv if ic == 0 else (),
                                  inc=("pe", 1) if ic == NDC - 1 else None, tag="b_v")
                    vdone = P.op("dve", "tensor_copy", vrow[s][:, hf * 512:hf * 512 + 512],
                                 bv, waits=[("pe", pv)] +
                                 (qrel.get(("v", s), []) if hf == 0 else []),
                                 inc=("dve", 1), tag="b_vc")
                    qrel[("vb", (tb * 2 + hf) % 2)] = [("dve", vdone)]
                sv = qst.load("pool", vS[t0 + tb * 128: t0 + tb * 128 + 128, :],
                              vrow[s][:, :], waits=[("dve", vdone)])
                qrel[("v", s)] = [sv]
                drain(1)
            pv_guard[tc] = pv
            drain(4)
        drain(99)
        P.barrier()
        if qkv_only:
            return

        strip = A.view(0, 128, 32 * 512, BF16)
        qtc = A.view(32 * 1024, 128, NDC * 512, BF16)
        kbf = [A.view(40 * 1024 + s * 8192, 128, 4096, BF16) for s in range(2)]
        vbf = [A.view(104 * 1024 + s * 8192, 128, 4096, BF16) for s in range(2)]
        wtb = [A.view(60 * 1024 + s * 2048, 128, 512, F32) for s in range(2)]
        wor = A.view(64 * 1024, 128, NDC * 1024, BF16)
        retrT = A.view(80 * 1024, 128, NDC * 512, BF16)
        ttf = [A.view(88 * 1024 + s * 2048, 128, 512, F32) for s in range(2)]
        otf = [A.view(92 * 1024 + s * 2048, 128, 512, F32) for s in range(2)]
        xf2 = [A.view(96 * 1024 + s * 2048, 128, 512, F32) for s in range(2)]
        tmpw = [A.view(100 * 1024 + s * 2048, 128, 512, F32) for s in range(2)]

        P.sem("bw2")
        for ic in range(NDC):
            P.op("sp", "dma_start", out=wor[:, ic * 1024:(ic + 1) * 1024],
                 in_=wobs[ic * 128:(ic + 1) * 128, :], inc=("bw2", 16), tag="bwo")
        BW2 = [("bw2", P.cnt["bw2"])]

        ot_rel = {}
        xr_rel = {}
        kb_rel = {}
        vb_rel = {}
        wt_rel = {}
        strip_rel = {}
        for tc in range(8):
            t0 = tc * 512
            sb0 = 4 * tc
            nsb = 32 - sb0
            lq = ql.load("sp", qtc.rearrange("p (a t) -> p a t", t=512),
                         qS[:, :, t0:t0 + 512].rearrange("a p t -> p a t"),
                         waits=[("pe", P.cnt["pe"])])
            strip_done = None
            sd_posts = {}
            for j4 in range((nsb + 3) // 4):
                sbs = list(range(sb0 + j4 * 4, min(sb0 + (j4 + 1) * 4, 32)))
                slen = len(sbs)
                s = j4 % 2
                lk = kl.load(
                    "sp",
                    kbf[s][:, 0:NDC * slen * 128].rearrange("p (a t) -> p a t", a=NDC),
                    kS[:, :, sbs[0] * 128: sbs[0] * 128 + slen * 128].rearrange(
                        "a p t -> p a t"),
                    waits=kb_rel.get(s, []))
                pk = None
                for jj, sb in enumerate(sbs):
                    pp = bank(sb % 2)
                    delta = sb * 128 - t0
                    for ic in range(NDC):
                        pk = P.op("pe", "matmul", pp,
                                  kbf[s][:, ic * slen * 128 + jj * 128:
                                         ic * slen * 128 + jj * 128 + 128],
                                  qtc[:, ic * 512:(ic + 1) * 512],
                                  start=(ic == 0), stop=(ic == NDC - 1),
                                  waits=[lq, lk] + strip_rel.get(sb % 2, [])
                                  if ic == 0 else (),
                                  inc=("pe", 1) if ic == NDC - 1 else None, tag="b_s")
                    ws = sb % 2
                    if delta >= 512:
                        wp = P.op("act", "activation", out=wtb[ws][:, :], in_=cio[:, :],
                                  func=AF.Exp, scale=lndb[:, 0:1],
                                  bias=ldt[:, delta // 128:delta // 128 + 1],
                                  waits=wt_rel.get(ws, []), inc=("act", 1), tag="b_w")
                        wpd = ("act", wp)
                    else:
                        P.op("dve", "tensor_scalar", tmpw[ws][:, :], cio[:, :],
                             float(delta - 1), lndb[:, 0:1], ALU.add, ALU.mult,
                             waits=wt_rel.get(ws, []), tag="b_w1")
                        dw2 = P.op("dve", "tensor_scalar_min", tmpw[ws][:, :],
                                   tmpw[ws][:, :], 0.0, inc=("dve", 1), tag="b_w2")
                        wp = P.op("act", "activation", out=wtb[ws][:, :],
                                  in_=tmpw[ws][:, :], func=AF.Exp,
                                  waits=[("dve", dw2)] + wt_rel.get(ws, []),
                                  inc=("act", 1), tag="b_w3")
                        c0 = 384 - delta
                        wpm = P.op("dve", "tensor_mul", wtb[ws][:, :], wtb[ws][:, :],
                                   t1m[:, c0:c0 + 512], waits=[("act", wp)],
                                   inc=("dve", 1), tag="b_w4")
                        wpd = ("dve", wpm)
                    sd = P.op("dve", "tensor_mul",
                              strip[:, (sb - sb0) * 512:(sb - sb0) * 512 + 512],
                              pp, wtb[ws][:, :], waits=[("pe", pk), wpd],
                              inc=("dve", 1), tag="b_p")
                    sd_posts[sb - sb0] = [("dve", sd)]
                    wt_rel[ws] = [("dve", sd)]
                    strip_rel[sb % 2] = [("dve", sd)]
                    strip_done = sd
                kb_rel[s] = [("pe", pk)]
            retr_stop = {}
            for jj in range(nsb):
                sb = sb0 + jj
                s = (jj // 4) % 2
                if jj % 4 == 0:
                    nv = min(4, nsb - jj)
                    lv = vl.load("sp",
                                 vbf[s][:, 0:nv * 1024].rearrange(
                                     "p (a t) -> p a t", a=nv),
                                 vS[sb * 128:sb * 128 + nv * 128, :].rearrange(
                                     "(a p) t -> p a t", p=128),
                                 waits=vb_rel.get(s, []))
                pr = None
                for oc in range(2, NDC):
                    pr = P.op("pe", "matmul", bank(oc),
                              vbf[s][:, (jj % 4) * 1024 + oc * 128:
                                     (jj % 4) * 1024 + (oc + 1) * 128],
                              strip[:, jj * 512:jj * 512 + 512],
                              start=(jj == 0), stop=(jj == nsb - 1),
                              waits=sd_posts[jj] + [lv] if oc == 2 else (),
                              inc=("pe", 1) if oc == NDC - 1 else None, tag="b_r")
                    if jj == nsb - 1:
                        retr_stop[oc] = pr
                if jj % 4 == 3 or jj == nsb - 1:
                    vb_rel[s] = [("pe", pr)]
            rv = None
            for oc in range(2, NDC):
                rv = P.op("dve", "tensor_copy", retrT[:, oc * 512:(oc + 1) * 512],
                          bank(oc), waits=[("pe", retr_stop[NDC - 1])],
                          inc=("dve", 1), tag="b_rc")
            for jj in range(nsb):
                sb = sb0 + jj
                s = (jj // 4) % 2
                if jj % 4 == 0:
                    nv = min(4, nsb - jj)
                    lv = vl.load("sp",
                                 vbf[s][:, 0:nv * 1024].rearrange(
                                     "p (a t) -> p a t", a=nv),
                                 vS[sb * 128:sb * 128 + nv * 128, :].rearrange(
                                     "(a p) t -> p a t", p=128),
                                 waits=vb_rel.get(s, []))
                pr = None
                for oc in range(2):
                    pr = P.op("pe", "matmul", bank(oc),
                              vbf[s][:, (jj % 4) * 1024 + oc * 128:
                                     (jj % 4) * 1024 + (oc + 1) * 128],
                              strip[:, jj * 512:jj * 512 + 512],
                              start=(jj == 0), stop=(jj == nsb - 1),
                              waits=[("dve", strip_done), lv] if oc == 0 else (),
                              inc=("pe", 1) if oc == 1 else None, tag="b_r2")
                    if jj == nsb - 1:
                        retr_stop[oc] = pr
                if jj % 4 == 3 or jj == nsb - 1:
                    vb_rel[s] = [("pe", pr)]
            for oc in range(2):
                rv = P.op("dve", "tensor_copy", retrT[:, oc * 512:(oc + 1) * 512],
                          bank(oc), waits=[("pe", retr_stop[1])],
                          inc=("dve", 1), tag="b_rc2")
            retr_done = rv
            ev = {}
            for ocO in range(NDC):
                s = ocO % 2
                bnk = bank(ocO % 4)
                lw = xq.load("pool", xf2[s][:, :],
                             src[ocO * 128:(ocO + 1) * 128, t0:t0 + 512],
                             waits=xr_rel.get(s, []))
                p1 = None
                for ic in range(NDC):
                    p1 = P.op("pe", "matmul", bnk,
                              wor[:, ic * 1024 + ocO * 128: ic * 1024 + ocO * 128 + 128],
                              retrT[:, ic * 512:(ic + 1) * 512],
                              start=(ic == 0), stop=(ic == NDC - 1),
                              waits=BW2 + [("dve", retr_done)] + ev.get(ocO % 4, [])
                              if ic == 0 else (),
                              inc=("pe", 1) if ic == NDC - 1 else None, tag="b_o")
                dp = P.op("dve", "tensor_add", otf[s][:, :], bnk, xf2[s][:, :],
                          waits=[("pe", p1), lw] + ot_rel.get(s, []),
                          inc=("dve", 1), tag="b_e")
                ev[ocO % 4] = [("dve", dp)]
                sw = st.load("pool", dst[ocO * 128:(ocO + 1) * 128, t0:t0 + 512],
                             otf[s][:, :], waits=[("dve", dp)])
                ot_rel[s] = [sw]
                xr_rel[s] = [("dve", dp)]
            P.after("pe", [("act", P.cnt["act"]), ("dve", P.cnt["dve"])])
        P.barrier()

    if nsub == 1:
        emit_attn_a(xin, yout)
    elif nsub == 2:
        emit_attn_a(xin, xr1)
        emit_ffn(xr1, yout)
    elif nsub == 30:
        emit_attn_a(xin, xr1)
        emit_ffn(xr1, xr2)
        emit_attn_b(xr2, yout, qkv_only=True)
    elif nsub == 3:
        emit_attn_a(xin, xr1)
        emit_ffn(xr1, xr2)
        emit_attn_b(xr2, yout)
    else:
        emit_attn_a(xin, xr1)
        emit_ffn(xr1, xr2)
        emit_attn_b(xr2, xr3)
        emit_ffn(xr3, yout)

    fin = [(s, P.cnt[s]) for s in P.cnt if s.startswith("st") and P.cnt[s] > 0]
    P.sem("fz")
    P.op("pool", "dma_start", out=xr1[0:1, 0:1], in_=yout[0:1, 0:1],
         waits=fin, inc=("fz", 16), tag="fin")

    from contextlib import ExitStack
    with ExitStack() as es:
        sems = {name: es.enter_context(nc.semaphore(name)) for name in P.cnt}
        block = es.enter_context(nc.Block())
        block.sync(lambda e: P.replay("sp", e, sems))
        block.scalar(lambda e: P.replay("act", e, sems))
        block.vector(lambda e: P.replay("dve", e, sems))
        block.tensor(lambda e: P.replay("pe", e, sems))
        block.gpsimd(lambda e: P.replay("pool", e, sems))
    return nc



def _tile_oc(WT, nic, noc):
    return np.ascontiguousarray(
        WT.reshape(nic, 128, noc, 128).transpose(2, 1, 0, 3).reshape(noc, 128, nic * 128))


def _prep_common(inp):
    f32 = np.float32

    def bd(w):
        out = np.zeros((128, 128), f32)
        wt = np.asarray(w, f32).T
        out[0:64, 0:64] = wt
        out[64:128, 64:128] = wt
        return out

    p = np.arange(128)[:, None]
    c = np.arange(896)[None, :]
    return {
        "wqa": bd(inp["L0_attn_a_Wq"]),
        "wka": bd(inp["L0_attn_a_Wk"]),
        "wva": bd(inp["L0_attn_a_Wv"]),
        "wut": np.ascontiguousarray(np.asarray(inp["L0_attn_a_Wu"], f32).T),
        "but": np.ascontiguousarray(np.asarray(inp["L0_attn_a_bu"], f32).reshape(NDC, 128).T),
        "wqbt": np.ascontiguousarray(np.asarray(inp["L0_attn_b_Wq"], f32).T),
        "wkbt": np.ascontiguousarray(np.asarray(inp["L0_attn_b_Wk"], f32).T),
        "wvbt": np.ascontiguousarray(np.asarray(inp["L0_attn_b_Wv"], f32).T),
        "wobt": np.ascontiguousarray(np.asarray(inp["L0_attn_b_Wo"], f32).T),
        "dlog": np.asarray(inp["L0_attn_b_decay_logit"], f32).reshape(1, 1),
        "oscl": np.asarray(inp["L0_attn_b_out_scale"], f32).reshape(1, 1),
        "w1t": _tile_oc(np.asarray(inp["L0_ffn_w1_w"], f32).T, NDC, NHC),
        "w2t": _tile_oc(np.asarray(inp["L0_ffn_w2_w"], f32).T, NDC, NHC),
        "h1t": _tile_oc(np.asarray(inp["L0_ffn_h1_w"], f32).T, NHC, NHC),
        "h2t": _tile_oc(np.asarray(inp["L0_ffn_h2_w"], f32).T, NHC, NHC),
        "w3t": _tile_oc(np.asarray(inp["L0_ffn_w3_w"], f32).T, NHC, NDC),
        "s1v": np.ascontiguousarray(np.asarray(inp["L0_ffn_w1_s"], f32).reshape(NHC, 128).T),
        "s2v": np.ascontiguousarray(np.asarray(inp["L0_ffn_w2_s"], f32).reshape(NHC, 128).T),
        "sh1v": np.ascontiguousarray(np.asarray(inp["L0_ffn_h1_s"], f32).reshape(NHC, 128).T),
        "sh2v": np.ascontiguousarray(np.asarray(inp["L0_ffn_h2_s"], f32).reshape(NHC, 128).T),
        "s3v": np.ascontiguousarray(np.asarray(inp["L0_ffn_w3_s"], f32).reshape(NDC, 128).T),
        "m1c": (p > c - 384).astype(f32),
        "ioc": (np.arange(128)[:, None] - np.arange(512)[None, :]).astype(f32),
        "dm1": (128.0 * np.arange(32)[None, :] - 1.0).astype(f32),
        "on1": np.ones((1, 128), f32),
        "cstv": np.tile(np.array([[EPS, EXP_SHIFT, SCALE_A, SCALE_B]], f32), (128, 1)),
    }


def kernel(**inp):
    x = np.asarray(inp["L0_attn_a_x"], np.float32)
    xT = np.ascontiguousarray(x.transpose(0, 2, 1))
    common = _prep_common(inp)
    nc = build_program(NSUB)
    in_maps = [dict(common, xT=np.ascontiguousarray(xT[b])) for b in range(NCORES)]
    res = run_bass_kernel_spmd(nc, in_maps, core_ids=list(range(NCORES)))
    y = np.stack([np.asarray(r["y"]) for r in res.results])
    return np.ascontiguousarray(y.transpose(0, 2, 1)).astype(np.float32)
